# version 52
# baseline (speedup 1.0000x reference)
import numpy as np
import ml_dtypes
import concourse.bass as bass
import concourse.mybir as mybir
from concourse.bass_utils import run_bass_kernel_spmd

F32 = mybir.dt.float32
BF16 = mybir.dt.bfloat16
I32 = mybir.dt.int32
ALU = mybir.AluOpType
AF = mybir.ActivationFunctionType
AX = mybir.AxisListType

E, H, D, CH = 1024, 16, 64, 128
T = 256
NTOK = 8192
NCORES = 8
EPS = 1.0e-5
RSQRT_MAGIC = 0x5F3759DF



def host_tables():
    s = np.array([2.0 ** (-(i + 1) / 2.0) for i in range(H)], dtype=np.float32)
    s = s * np.float32(1.0 + 1e-5)
    idx = np.arange(CH, dtype=np.float32)
    diff = idx[:, None] - idx[None, :]
    mask = np.where(diff >= 0, np.exp(-s[:, None, None] * diff), 0.0).astype(np.float32)
    lam_q = np.exp(-s[:, None] * (idx + 1.0)).astype(np.float32)
    lam_k = np.exp(-s[:, None] * (CH - 1.0 - idx)).astype(np.float32)
    lam_c = np.exp(-s * CH).astype(np.float32)

    bf = ml_dtypes.bfloat16
    HEAD_ORDER = [0, 2, 4, 6, 1, 3, 5, 7, 8, 10, 12, 14, 9, 11, 13, 15]
    maskT = np.ascontiguousarray(mask.transpose(2, 0, 1)[:, HEAD_ORDER, :]).astype(bf)
    p = np.arange(128)
    fc = np.arange(8)
    hh = 2 * fc[None, :] + (p[:, None] >= 64)
    lamq = lam_q[hh][:, :, :].astype(bf)
    dd = np.arange(128)
    hh2 = 2 * fc[:, None] + (dd[None, :] >= 64)
    lamk = np.ascontiguousarray(lam_k[hh2].transpose(2, 0, 1)).astype(bf)
    hp = np.arange(8)
    hh3 = 2 * hp[None, :] + (p[:, None] >= 64)
    lamc_bc = lam_c[hh3].astype(np.float32)
    ident = np.eye(128, dtype=bf)
    return {"maskT": maskT, "lamq": lamq, "lamk": lamk, "lamc_bc": lamc_bc, "ident": ident}



class Sched:
    COMPUTE = ("pe", "act", "dve", "pool")
    QUEUES = ("syncq", "poolq", "actq")
    STREAM_OF = {"pe": "pe", "act": "act", "dve": "dve", "pool": "pool",
                 "syncq": "sync", "poolq": "pool", "actq": "act"}

    def __init__(self):
        self.progs = {s: [] for s in ("pe", "act", "dve", "pool", "sync")}
        self.cnt = {e: 0 for e in list(self.COMPUTE) + list(self.QUEUES)}
        self.writer = {}
        self.readers = {}
        self.waited = {}
        self.sems = {}
        self.nwaits = 0

    def _want(self, wants, waiter, src, val, src_small=False):
        if val <= 0:
            return
        if self.STREAM_OF[src] == self.STREAM_OF[waiter]:
            if not src_small:
                return
        if self.waited.get((waiter, src), 0) >= val:
            return
        wants[src] = max(wants.get(src, 0), val)

    def _flush_waits(self, waiter, wants):
        for src, val in wants.items():
            self.waited[(waiter, src)] = val
            self.progs[self.STREAM_OF[waiter]].append(
                lambda eng, s=src, v=val: eng.wait_ge(self.sems[s], v))
            self.nwaits += 1

    def op(self, eng, fn, reads=(), writes=(), inc=True, small=False):
        wants = {}
        for b in reads:
            w = self.writer.get(b)
            if w:
                self._want(wants, eng, w[0], w[1], w[2] if len(w) > 2 else False)
        for b in writes:
            w = self.writer.get(b)
            if w:
                self._want(wants, eng, w[0], w[1], w[2] if len(w) > 2 else False)
            for r_eng, r_cnt in self.readers.get(b, {}).items():
                self._want(wants, eng, r_eng, r_cnt)
        self._flush_waits(eng, wants)
        is_dma = eng in self.QUEUES
        step = 16 if is_dma else (1 if inc else 0)
        self.cnt[eng] += step
        val = self.cnt[eng] if step else self.cnt[eng] + 1
        if step:
            self.progs[self.STREAM_OF[eng]].append(
                lambda h, fn=fn, e=eng, s=step: fn(h).then_inc(self.sems[e], s))
        else:
            self.progs[self.STREAM_OF[eng]].append(lambda h, fn=fn: fn(h))
        for b in reads:
            self.readers.setdefault(b, {})
            prev = self.readers[b].get(eng, 0)
            self.readers[b][eng] = max(prev, val)
        if small:
            assert step == 1, "small ops must inc"
        for b in writes:
            self.writer[b] = (eng, val, small)
            self.readers[b] = {}
        return val

    def raw(self, stream, fn):
        self.progs[stream].append(fn)



def build_program(ntiles=NTOK // T, debug=False, stage_limit=99, skip=()):
    nc = bass.Bass("TRN2", target_bir_lowering=False, detect_race_conditions=False)
    ntok = ntiles * T

    x_h = nc.dram_tensor("x", [ntok, E], F32, kind="ExternalInput").ap()
    out_h = nc.dram_tensor("out", [ntok, E], F32, kind="ExternalOutput").ap()
    wq_h = nc.dram_tensor("Wq", [E, E], F32, kind="ExternalInput").ap()
    wk_h = nc.dram_tensor("Wk", [E, E], F32, kind="ExternalInput").ap()
    wv_h = nc.dram_tensor("Wv", [E, E], F32, kind="ExternalInput").ap()
    wo_h = nc.dram_tensor("Wo", [E, E], F32, kind="ExternalInput").ap()
    wg1_h = nc.dram_tensor("Wg1", [E, D], F32, kind="ExternalInput").ap()
    wg2_h = nc.dram_tensor("Wg2", [128, E], F32, kind="ExternalInput").ap()
    wr_h = nc.dram_tensor("W_r", [E, E], F32, kind="ExternalInput").ap()
    wk2_h = nc.dram_tensor("W_k2", [E, E], F32, kind="ExternalInput").ap()
    wv2_h = nc.dram_tensor("W_v2", [E, E], F32, kind="ExternalInput").ap()
    gamma_h = nc.dram_tensor("gammab", [128, E], F32, kind="ExternalInput").ap()
    beta_h = nc.dram_tensor("betab", [128, E], F32, kind="ExternalInput").ap()
    mur_h = nc.dram_tensor("mur", [128, 8], F32, kind="ExternalInput").ap()
    muk_h = nc.dram_tensor("muk", [128, 8], F32, kind="ExternalInput").ap()
    maskT_h = nc.dram_tensor("maskT", [128, 16, 128], BF16, kind="ExternalInput").ap()
    lamq_h = nc.dram_tensor("lamq", [128, 8, 128], BF16, kind="ExternalInput").ap()
    lamk_h = nc.dram_tensor("lamk", [128, 8, 128], BF16, kind="ExternalInput").ap()
    lamc_h = nc.dram_tensor("lamc_bc", [128, 8], F32, kind="ExternalInput").ap()
    ident_h = nc.dram_tensor("ident", [128, 128], BF16, kind="ExternalInput").ap()

    dbg = {}
    if debug:
        for nm, shape, dt in [
            ("xhat1_tm", [128, 2, E], BF16), ("xhat1_fm", [128, 8, T], BF16),
            ("q", [128, 8, T], BF16), ("k", [128, 8, T], BF16),
            ("v", [128, 2, E], BF16), ("gate", [128, 2, E], BF16),
            ("o", [128, 2, E], F32), ("og", [128, 2, E], BF16),
            ("y", [128, 2, E], F32), ("xnew", [128, 2, E], F32),
            ("ln3", [128, 8, T + 1], BF16), ("xr", [128, 8, T], BF16),
            ("xk", [128, 8, T], BF16), ("r", [128, 2, E], BF16),
            ("kact", [128, 8, T], BF16), ("y2f", [128, 2, E], F32),
            ("ktil0", [128, 8, 128], BF16), ("qhat0", [128, 8, 128], BF16),
            ("sm0", [128, 2, 4, 128], BF16), ("kv0", [128, 8, D], F32),
        ]:
            dbg[nm] = nc.dram_tensor("dbg_" + nm, shape, dt, kind="ExternalOutput").ap()

    x_tiles = x_h.rearrange("(n g p) e -> n p g e", p=128, g=2)
    out_tiles = out_h.rearrange("(n g p) e -> n p g e", p=128, g=2)

    sch = Sched()

    from contextlib import ExitStack
    with ExitStack() as _stack:
        def SB(name, shape, dt):
            return _stack.enter_context(nc.sbuf_tensor(name, shape, dt))

        def PS(name, shape, dt):
            return _stack.enter_context(nc.psum_tensor(name, shape, dt))

        def SEM(name):
            return _stack.enter_context(nc.semaphore(name))

        x0 = SB("x0", [128, 2, E], F32)
        x1 = SB("x1", [128, 2, E], F32)
        x2 = SB("x2", [128, 2, E], F32)
        OBF = SB("OBF", [128, 2, E], BF16)
        R2b = SB("R2b", [128, 2, E], BF16)
        kvb = SB("kv", [128, 8, D], F32)
        statsb = SB("stats", [128, 4, 2, 4, 6], F32)
        mvb = SB("mv", [128, 4, 2, 2], F32)
        ncolb = SB("ncol", [128, 4, 4, 2], F32)
        halfb = SB("halfb", [128, 1], F32)
        lamc_s = SB("lamc_s", [128, 8], F32)
        mur_s = SB("mur_s", [128, 8], F32)
        muk_s = SB("muk_s", [128, 8], F32)
        A1 = SB("A1", [128, 2, E], BF16)
        A2 = SB("A2", [128, 8, T], BF16)
        A3 = SB("A3", [128, 8, T], BF16)
        A4 = SB("A4", [128, 8, T], BF16)
        A5 = SB("A5", [128, 2, E], BF16)
        A6 = SB("A6", [128, 8, CH], BF16)
        A7 = SB("A7", [128, 2, E], BF16)
        A8 = SB("A8", [128, 2, E], BF16)
        A9 = SB("A9", [128, 2, E], BF16)
        ktil = SB("ktil", [128, 8, 128], BF16)
        smb = SB("sm", [128, 2, 2, 4, 128], BF16)
        kvbf = SB("kvbf", [128, 8, D], BF16)
        g64b = SB("g64", [128, T], BF16)
        wq = SB("wq", [128, 8, E], BF16)
        wk = SB("wk", [128, 8, E], BF16)
        wv = SB("wv", [128, 8, E], BF16)
        wo = SB("wo", [128, 8, E], BF16)
        wrT = SB("wrT", [128, 8, E], BF16)
        wkT = SB("wkT", [128, 8, E], BF16)
        wvT = SB("wvT", [128, 8, E], BF16)
        wg1 = SB("wg1", [128, 8, D], BF16)
        wg2 = SB("wg2", [128, E], BF16)
        gam_s = SB("gammab_s", [128, E], BF16)
        bet_s = SB("betab_s", [128, E], BF16)
        maskT_s = SB("maskT_s", [128, 16, 128], BF16)
        lamq_s = SB("lamq_s", [128, 8, 128], BF16)
        lamk_s = SB("lamk_s", [128, 8, 128], BF16)
        ident_s = SB("ident_s", [128, 128], BF16)
        ln3fm = SB("ln3fm", [128, 8, T + 1], BF16)
        PF0 = PS("PF0", [128, 512], F32)
        PF1 = PS("PF1", [128, 512], F32)
        PF2 = PS("PF2", [128, 512], F32)
        PF3 = PS("PF3", [128, 512], F32)
        PF4 = PS("PF4", [128, 512], F32)
        PF5 = PS("PF5", [128, 512], F32)
        PT0 = PS("PT0", [128, 8, 128], BF16)
        PT1 = PS("PT1", [128, 8, 128], BF16)
        s_pe = SEM("s_pe")
        s_act = SEM("s_act")
        s_dve = SEM("s_dve")
        s_pool = SEM("s_pool")
        s_syncq = SEM("s_syncq")
        s_poolq = SEM("s_poolq")
        s_actq = SEM("s_actq")
        block = _stack.enter_context(nc.Block())

        sch.sems = {"pe": s_pe, "act": s_act, "dve": s_dve, "pool": s_pool,
                    "syncq": s_syncq, "poolq": s_poolq, "actq": s_actq}
        def tap(nm, buf_ap, key):
            if debug and nm in dbg:
                sch.op("syncq", lambda s, d=dbg[nm], b=buf_ap: s.dma_start(out=d[:], in_=b),
                       reads=[key], writes=[])
                del dbg[nm]

        xs = [x0, x1, x2]
        PF = [PF0, PF1, PF2, PF3, PF4, PF5]
        PT = [PT0, PT1]
        fkey = ["PF0", "PF1", "PF2", "PF3", "PF4", "PF5"]
        tkey = ["PT0", "PT1"]

        for name, dst, src in [("maskT", maskT_s, maskT_h), ("lamq", lamq_s, lamq_h),
                               ("lamk", lamk_s, lamk_h), ("ident", ident_s, ident_h),
                               ("lamc", lamc_s, lamc_h), ("mur", mur_s, mur_h),
                               ("muk", muk_s, muk_h)]:
            sch.op("poolq", lambda g, d=dst, s=src: g.dma_start(out=d[:], in_=s[:]),
                   writes=[name])
        sch.op("pool", lambda g: g.memset(kvb[:], 0.0), writes=["kv"])
        sch.op("pool", lambda g: g.memset(halfb[:], 0.5), writes=["half"])
        sch.op("pool", lambda g: g.memset(kvbf[:], 0.0), writes=["kvbf"])
        sch.op("pool", lambda g: g.memset(ln3fm[:, :, 0:1], 0.0), writes=["ln3"])

        sch.op("syncq", lambda s: s.dma_start(out=x0[:], in_=x_tiles[0]), writes=["x0"])

        stage = x1[:].rearrange("p g e -> p (g e)")
        for name, dsts, srcs in [("gamma", gam_s, gamma_h), ("beta", bet_s, beta_h)]:
            sch.op("syncq", lambda s2, d=stage, s=srcs: s2.dma_start(out=d[:, 0:E], in_=s[:]),
                   reads=(), writes=["x1"])
            sch.op("pool", lambda g, d=dsts, s=stage: g.tensor_copy(out=d[:], in_=s[:, 0:E]),
                   reads=["x1"], writes=[name])

        stage2 = x2[:].rearrange("p g e -> p (g e)")
        wph = [0]

        def _wstage():
            p = wph[0] % 2
            wph[0] += 1
            return (stage, "x1", "syncq") if p == 0 else (stage2, "x2", "actq")

        def load_weight(dst, src_h, key):
            for ic2 in range(4):
                stg, skey, q = _wstage()
                half = src_h[ic2 * 256:(ic2 + 1) * 256, :].rearrange("(a p) e -> p a e", p=128)
                sch.op(q, lambda s, d=stg, hh=half:
                       s.dma_start(out=d[:].rearrange("p (a e) -> p a e", a=2), in_=hh),
                       writes=[skey])
                for eng, lo in (("pool", 0), ("dve", E)):
                    sch.op(eng, lambda g, d=dst, i=ic2, stg=stg, lo=lo:
                           g.tensor_copy(out=d[:, 2 * i + lo // E, :],
                                         in_=stg[:, lo:lo + E]),
                           reads=[skey], writes=[key])

        load_weight(wq, wq_h, "wq")
        load_weight(wk, wk_h, "wk")
        load_weight(wv, wv_h, "wv")
        sch.op("syncq", lambda s: s.dma_start(
            out=stage[:, 0:512].rearrange("p (a e) -> p a e", a=8),
            in_=wg1_h[:].rearrange("(a p) e -> p a e", p=128)), writes=["x1"])
        sch.op("pool", lambda g: g.tensor_copy(
            out=wg1[:], in_=stage[:, 0:512].rearrange("p (a e) -> p a e", a=8)),
            reads=["x1"], writes=["wg1"])
        sch.op("syncq", lambda s: s.dma_start(out=stage[:, 0:E], in_=wg2_h[:]),
               writes=["x1"])
        sch.op("pool", lambda g: g.tensor_copy(out=wg2[:], in_=stage[:, 0:E]),
               reads=["x1"], writes=["wg2"])
        load_weight(wo, wo_h, "wo")

        cast1 = A5[:].rearrange("p g e -> p (g e)")
        cast2 = A3[:].rearrange("p g e -> p (g e)")

        def load_weight_T(dstT, src_h, key):
            for ic2 in range(4):
                stg, skey, q = _wstage()
                cast, ckey = (cast1, "A5") if skey == "x1" else (cast2, "A3")
                half = src_h[ic2 * 256:(ic2 + 1) * 256, :].rearrange("(a p) e -> p a e", p=128)
                sch.op(q, lambda s, d=stg, hh=half:
                       s.dma_start(out=d[:].rearrange("p (a e) -> p a e", a=2), in_=hh),
                       writes=[skey])
                for eng, lo in (("pool", 0), ("dve", E)):
                    sch.op(eng, lambda g, stg=stg, cast=cast, lo=lo:
                           g.tensor_copy(out=cast[:, lo:lo + E], in_=stg[:, lo:lo + E]),
                           reads=[skey], writes=[ckey])
                for a in range(2):
                    dcc = ic2 * 2 + a
                    pt = PT[dcc % 2]
                    ptk = tkey[dcc % 2]
                    for cc in range(8):
                        last = (cc == 7)
                        sch.op("pe", lambda t, a=a, cc=cc, cast=cast, pt=pt:
                               t.transpose(out=pt[:, cc, :],
                                           in_=cast[:, a * E + cc * 128:a * E + (cc + 1) * 128],
                                           identity=ident_s[:]),
                               reads=[ckey, "ident"], writes=[ptk], inc=last)
                    sch.op("dve", lambda v, d=dstT, dcc=dcc, pt=pt:
                           v.tensor_copy(out=d[:, :, dcc * 128:(dcc + 1) * 128],
                                         in_=pt[:]),
                           reads=[ptk], writes=[key])

        load_weight_T(wrT, wr_h, "wrT")
        load_weight_T(wkT, wk2_h, "wkT")
        load_weight_T(wvT, wv2_h, "wvT")

        def ln_stats(slot, tg, q, src2d, src_key):
            sch.op("dve", lambda v, s=slot, tg=tg, q=q, src=src2d:
                   v.bn_stats(out=statsb[:, s, tg, q, :], in_=src),
                   reads=[src_key], writes=[("st", slot, tg, q)], small=True)

        def ln_finish(slot):
            st = statsb[:, slot]
            mv = mvb[:, slot]
            ncol = ncolb[:, slot]
            u, y, t2 = ncol[:, 0], ncol[:, 1], ncol[:, 2]
            for tg in range(2):
                sch.op("dve", lambda v, tg=tg, st=st, mv=mv: v.bn_aggr(out=mv[:, tg, :], in_=st[:, tg]),
                       reads=[("st", slot, tg, q) for q in range(4)],
                       writes=[("mv", slot)], small=True)
            mvv = mvb[:, slot, :, 1]
            sch.op("dve", lambda v, u=u, mvv=mvv: v.tensor_single_scalar(out=u, in_=mvv, scalar=EPS, op=ALU.add),
                   reads=[("mv", slot)], writes=[("u", slot)], small=True)
            sch.op("dve", lambda v, u=u, y=y: v.tensor_single_scalar(
                out=y.bitcast(I32), in_=u.bitcast(I32), scalar=1, op=ALU.logical_shift_right),
                reads=[("u", slot)], writes=[("y", slot)], small=True)
            sch.op("dve", lambda v, y=y: v.tensor_scalar(
                out=y.bitcast(I32), in0=y.bitcast(I32), scalar1=-1, scalar2=RSQRT_MAGIC,
                op0=ALU.mult, op1=ALU.add),
                reads=[("y", slot)], writes=[("y", slot)], small=True)
            for _ in range(1):
                sch.op("dve", lambda v, y=y, t2=t2: v.tensor_mul(out=t2, in0=y, in1=y),
                       reads=[("y", slot)], writes=[("t2", slot)], small=True)
                sch.op("dve", lambda v, u=u, t2=t2: v.tensor_mul(out=t2, in0=t2, in1=u),
                       reads=[("t2", slot), ("u", slot)], writes=[("t2", slot)], small=True)
                sch.op("dve", lambda v, t2=t2: v.tensor_scalar(
                    out=t2, in0=t2, scalar1=-0.5, scalar2=1.5, op0=ALU.mult, op1=ALU.add),
                    reads=[("t2", slot)], writes=[("t2", slot)], small=True)
                sch.op("dve", lambda v, y=y, t2=t2: v.tensor_mul(out=y, in0=y, in1=t2),
                       reads=[("y", slot), ("t2", slot)], writes=[("y", slot)], small=True)

        def ln_finish2(sa, sb):
            for slot in (sa, sb):
                st = statsb[:, slot]
                mv = mvb[:, slot]
                for tg in range(2):
                    sch.op("dve", lambda v, tg=tg, st=st, mv=mv: v.bn_aggr(out=mv[:, tg, :], in_=st[:, tg]),
                           reads=[("st", slot, tg, q) for q in range(4)],
                           writes=[("mv", slot)], small=True)
            u = ncolb[:, sa:sa + 2, 0, :]
            y = ncolb[:, sa:sa + 2, 1, :]
            t2 = ncolb[:, sa:sa + 2, 2, :]
            mvv = mvb[:, sa:sa + 2, :, 1]
            rd = [("mv", sa), ("mv", sb)]
            wU = [("u", sa), ("u", sb)]
            wY = [("y", sa), ("y", sb)]
            wT = [("t2", sa), ("t2", sb)]
            sch.op("dve", lambda v, u=u, mvv=mvv: v.tensor_single_scalar(out=u, in_=mvv, scalar=EPS, op=ALU.add),
                   reads=rd, writes=wU, small=True)
            sch.op("dve", lambda v, u=u, y=y: v.tensor_single_scalar(
                out=y.bitcast(I32), in_=u.bitcast(I32), scalar=1, op=ALU.logical_shift_right),
                reads=wU, writes=wY, small=True)
            sch.op("dve", lambda v, y=y: v.tensor_scalar(
                out=y.bitcast(I32), in0=y.bitcast(I32), scalar1=-1, scalar2=RSQRT_MAGIC,
                op0=ALU.mult, op1=ALU.add),
                reads=wY, writes=wY, small=True)
            sch.op("dve", lambda v, y=y, t2=t2: v.tensor_mul(out=t2, in0=y, in1=y),
                   reads=wY, writes=wT, small=True)
            sch.op("dve", lambda v, u=u, t2=t2: v.tensor_mul(out=t2, in0=t2, in1=u),
                   reads=wT + wU, writes=wT, small=True)
            sch.op("dve", lambda v, t2=t2: v.tensor_scalar(
                out=t2, in0=t2, scalar1=-0.5, scalar2=1.5, op0=ALU.mult, op1=ALU.add),
                reads=wT, writes=wT, small=True)
            sch.op("dve", lambda v, y=y, t2=t2: v.tensor_mul(out=y, in0=y, in1=t2),
                   reads=wY + wT, writes=wY, small=True)

        def ln_apply(slot, tg, out_fn, reads, out_key):
            m_ap = mvb[:, slot, tg, 0:1]
            r_ap = ncolb[:, slot, 1, tg:tg + 1]
            ok = out_key if isinstance(out_key, list) else [out_key]
            sch.op("dve", lambda v, tg=tg, m=m_ap, r=r_ap: out_fn(v, tg, m, r),
                   reads=list(reads) + [("mv", slot), ("y", slot)], writes=ok)

        ln_counter = [0]
        ln1_slots = {}
        sl2_slots = {}

        def new_slot():
            s = ln_counter[0] % 4
            ln_counter[0] += 1
            return s

        def ln1_prestats(j):
            xt = xs[j % 3]
            xkey = f"x{j % 3}"
            sl = new_slot()
            ln1_slots[j] = sl
            for tg in range(2):
                for q in range(4):
                    ln_stats(sl, tg, q, xt[:, tg, q * 256:(q + 1) * 256], xkey)
            ln_finish(sl)

        def ln1_apply(j):
            xt = xs[j % 3]
            xkey = f"x{j % 3}"
            sl = ln1_slots.pop(j)

            def ln1_out(v, tg, m, r):
                return v.tensor_scalar(out=A1[:, tg, :], in0=xt[:, tg, :],
                                       scalar1=m, scalar2=r, op0=ALU.subtract, op1=ALU.mult)
            for tg in range(2):
                ln_apply(sl, tg, ln1_out, [xkey], "A1")

        def stage_a_ln(j, do_apply=True):
            if do_apply:
                ln1_apply(j)
            for tg in range(2):
                for fb in range(8):
                    sch.op("pe", lambda t, tg=tg, fb=fb:
                           t.transpose(out=PT[tg][:, fb, :],
                                       in_=A1[:, tg, fb * 128:(fb + 1) * 128],
                                       identity=ident_s[:]),
                           reads=["A1", "ident"], writes=[tkey[tg]], inc=(fb == 7))
            for tg in range(2):
                sch.op("dve", lambda v, tg=tg:
                       v.tensor_copy(out=A2[:, :, tg * 128:(tg + 1) * 128], in_=PT[tg][:]),
                       reads=[tkey[tg]], writes=["A2"])

        def stage_a(j):
            def proj_fm(wgt, wkeyn, dst, dkey, act):
                for dp in range(4):
                    bank = PF[dp % 2]
                    bkey = fkey[dp % 2]
                    for dc in (2 * dp, 2 * dp + 1):
                        for fc2 in range(8):
                            sch.op("pe", lambda t, dc=dc, fc=fc2, b=bank:
                                   t.matmul(out=b[:, (dc % 2) * 256:(dc % 2) * 256 + 256],
                                            lhsT=wgt[:, fc, dc * 128:(dc + 1) * 128],
                                            rhs=A2[:, fc, :],
                                            start=(fc == 0), stop=(fc == 7)),
                                   reads=[wkeyn, "A2"], writes=[bkey],
                                   inc=(fc2 == 7 and dc % 2 == 1))
                    sch.op("act", lambda s, dp=dp, b=bank, d=dst:
                           s.activation(out=d[:, 2 * dp:2 * dp + 2, :],
                                        in_=b[:].rearrange("p (a b) -> p a b", a=2), func=act),
                           reads=[bkey], writes=[dkey])

            proj_fm(wq, "wq", A3, "A3", AF.Silu)
            proj_fm(wk, "wk", A4, "A4", AF.Silu)
            for tg in range(2):
                for half in range(2):
                    bank = PF[2 + (tg * 2 + half) % 2]
                    bkey = fkey[2 + (tg * 2 + half) % 2]
                    for fc2 in range(8):
                        sch.op("pe", lambda t, tg=tg, h=half, fc=fc2, b=bank:
                               t.matmul(out=b[:],
                                        lhsT=A2[:, fc, tg * 128:(tg + 1) * 128],
                                        rhs=wv[:, fc, h * 512:(h + 1) * 512],
                                        start=(fc == 0), stop=(fc == 7)),
                               reads=["wv", "A2"], writes=[bkey], inc=(fc2 == 7))
                    sch.op("act", lambda s, tg=tg, h=half, b=bank:
                           s.activation(out=A5[:, tg, h * 512:(h + 1) * 512], in_=b[:], func=AF.Silu),
                           reads=[bkey], writes=["A5"])
            for fc2 in range(8):
                sch.op("pe", lambda t, fc=fc2:
                       t.matmul(out=PF4[0:64, 0:256], lhsT=wg1[:, fc, :], rhs=A2[:, fc, :],
                                start=(fc == 0), stop=(fc == 7)),
                       reads=["wg1", "A2"], writes=["PF4"], inc=(fc2 == 7))
            sch.op("act", lambda s: s.copy(out=g64b[0:64, :], in_=PF4[0:64, 0:256]),
                   reads=["PF4"], writes=["g64"])

        def stage_a2(j):
            for tg in range(2):
                for half in range(2):
                    bank = PF[(tg * 2 + half) % 2]
                    bkey = fkey[(tg * 2 + half) % 2]
                    sch.op("pe", lambda t, tg=tg, h=half, b=bank:
                           t.matmul(out=b[:], lhsT=g64b[0:64, tg * 128:(tg + 1) * 128],
                                    rhs=wg2[0:64, h * 512:(h + 1) * 512], start=True, stop=True),
                           reads=["g64", "wg2"], writes=[bkey])
                    sch.op("act", lambda s, tg=tg, h=half, b=bank:
                           s.activation(out=A7[:, tg, h * 512:(h + 1) * 512], in_=b[:],
                                        func=AF.Tanh, scale=0.5),
                           reads=[bkey], writes=["A7"])
                    sch.op("act", lambda s, tg=tg, h=half:
                           s.activation(out=A7[:, tg, h * 512:(h + 1) * 512],
                                        in_=A7[:, tg, h * 512:(h + 1) * 512],
                                        func=AF.Identity, scale=0.5, bias=halfb[:]),
                           reads=["A7", "half"], writes=["A7"])

        st_batches = [[8 * (b4 // 2) + 2 * ii + (b4 % 2) for ii in range(4)] for b4 in range(4)]
        st_banks = (2, 3, 0, 1)

        def front_mask(b4):
            sb = PF[st_banks[b4]]
            sbk = fkey[st_banks[b4]]
            sch.op("dve", lambda v, b4=b4, b=sb:
                   v.tensor_tensor(out=smb[:, b4 % 2, b4 // 2], in0=b[:].rearrange("p (a b) -> p a b", a=4),
                                   in1=maskT_s[:, b4 * 4:(b4 + 1) * 4, :], op=ALU.mult),
                   reads=[sbk, "maskT"], writes=[("sm", b4 % 2, b4 // 2)])

        def front_qk_dve(c):
            sch.op("dve", lambda v, c=c:
                   v.tensor_tensor(out=A6[:],
                                   in0=A3[:, :, c * 128:(c + 1) * 128],
                                   in1=lamq_s[:], op=ALU.mult),
                   reads=["A3", "lamq"], writes=["A6"])
            sch.op("dve", lambda v:
                   v.tensor_tensor(out=ktil[:], in0=PT0[:], in1=lamk_s[:], op=ALU.mult),
                   reads=["PT0", "lamk"], writes=["ktil"])

        def attn_front(i, c, defer=False):
            if defer is not True:
                sch.op("dve", lambda v, c=c:
                       v.tensor_tensor(out=A6[:],
                                       in0=A3[:, :, c * 128:(c + 1) * 128],
                                       in1=lamq_s[:], op=ALU.mult),
                       reads=["A3", "lamq"], writes=["A6"])
            for fb in range(8):
                sch.op("pe", lambda t, fb=fb, c=c:
                       t.transpose(out=PT0[:, fb, :], in_=A4[:, fb, c * 128:(c + 1) * 128],
                                   identity=ident_s[:]),
                       reads=["A4", "ident"], writes=["PT0"], inc=(fb == 7))
            for b4 in (2, 3, 0, 1):
                sb = PF[st_banks[b4]]
                sbk = fkey[st_banks[b4]]
                for hh in range(4):
                    h = st_batches[b4][hh]
                    fc2, po = h // 2, 64 * (h % 2)
                    sch.op("pe", lambda t, h=h, hh=hh, fc=fc2, po=po, b=sb, c=c:
                           t.matmul(out=b[:, hh * 128:(hh + 1) * 128],
                                    lhsT=A4[po:po + 64, fc, c * 128:(c + 1) * 128],
                                    rhs=A3[po:po + 64, fc, c * 128:(c + 1) * 128],
                                    start=True, stop=True),
                           reads=["A4", "A3"], writes=[sbk], inc=(hh == 3))
                if defer is not True:
                    front_mask(b4)
            if defer is False:
                sch.op("dve", lambda v:
                       v.tensor_tensor(out=ktil[:], in0=PT0[:], in1=lamk_s[:], op=ALU.mult),
                       reads=["PT0", "lamk"], writes=["ktil"])

        def attn_back(i, c, sl2, delay_key=None, tail=True):
            batches = [[8 * (b4 // 2) + 2 * ii + (b4 % 2) for ii in range(4)] for b4 in range(4)]
            first = [True]
            for b4 in range(4):
                for hh in range(4):
                    h = batches[b4][hh]
                    fc2, po = h // 2, 64 * (h % 2)
                    ob = PF[4 + h // 8]
                    obk = fkey[4 + h // 8]
                    extra = [delay_key] if (first[0] and delay_key) else []
                    first[0] = False
                    sch.op("pe", lambda t, h=h, hh=hh, b4=b4, b=ob, c=c:
                           t.matmul(out=b[:, (h % 8) * 64:(h % 8) * 64 + 64],
                                    lhsT=smb[:, b4 % 2, b4 // 2, hh, :],
                                    rhs=A5[:, c, h * 64:(h + 1) * 64],
                                    start=True, stop=False),
                           reads=[("sm", b4 % 2, b4 // 2), "A5"] + extra, writes=[obk], inc=False)
                    sch.op("pe", lambda t, h=h, fc=fc2, po=po, b=ob, c=c:
                           t.matmul(out=b[:, (h % 8) * 64:(h % 8) * 64 + 64],
                                    lhsT=A6[po:po + 64, fc, :],
                                    rhs=kvbf[po:po + 64, fc, :],
                                    start=False, stop=True),
                           reads=["A6", "kvbf"], writes=[obk],
                           inc=(h % 8 == 7))
            for h in range(16):
                fc2, po = h // 2, 64 * (h % 2)
                sch.op("pe", lambda t, h=h, fc=fc2, po=po, c=c:
                       t.matmul(out=PF0[po:po + 64, fc * 64:(fc + 1) * 64],
                                lhsT=ktil[:, fc, po:po + 64],
                                rhs=A5[:, c, h * 64:(h + 1) * 64],
                                start=True, stop=True),
                       reads=["ktil", "A5"], writes=["PF0"], inc=(h == 15))
            sch.op("pool", lambda g:
                   g.tensor_tensor(out=kvb[:], in0=kvb[:],
                                   in1=lamc_s[:].unsqueeze(2).broadcast_to((128, 8, D)), op=ALU.mult),
                   reads=["kv", "lamc"], writes=["kv"])
            sch.op("dve", lambda v:
                   v.tensor_tensor(out=kvb[:], in0=kvb[:],
                                   in1=PF0[:].rearrange("p (a b) -> p a b", a=8), op=ALU.add),
                   reads=["kv", "PF0"], writes=["kv"])
            if tail:
                attn_back_tail(c, sl2)

        def attn_back_tail(c, sl2):
            sch.op("pool", lambda g: g.tensor_copy(out=kvbf[:], in_=kvb[:]),
                   reads=["kv"], writes=["kvbf"])
            for half in range(2):
                b = PF[4 + half]
                bk = fkey[4 + half]
                sch.op("act", lambda s, c=c, h=half, b=b:
                       s.copy(out=OBF[:, c, h * 512:(h + 1) * 512], in_=b[:]),
                       reads=[bk], writes=["OBF"])
                for q in (2 * half, 2 * half + 1):
                    ln_stats(sl2, c, q, OBF[:, c, q * 256:(q + 1) * 256], "OBF")

        def tile(i):
            xsl = xs[i % 3]
            xkey = f"x{i % 3}"
            have1 = i + 1 < ntiles
            have2 = i + 2 < ntiles
            sl2 = sl2_slots.pop(i)

            if i == 0:
                ln_finish(sl2)

            def ln2_out(v, tg, m, r):
                return v.tensor_scalar(out=A8[:, tg, :], in0=OBF[:, tg, :],
                                       scalar1=m, scalar2=r, op0=ALU.subtract, op1=ALU.mult)
            for tg in range(2):
                ln_apply(sl2, tg, ln2_out, ["OBF"], "A8")
            H2 = E // 2
            for tg in range(2):
                for eng, lo in (("pool", 0), ("dve", H2)):
                    sch.op(eng, lambda x, tg=tg, lo=lo: x.tensor_tensor(
                        out=A9[:, tg, lo:lo + H2], in0=A8[:, tg, lo:lo + H2],
                        in1=gam_s[:, lo:lo + H2], op=ALU.mult),
                        reads=["A8", "gamma"], writes=["A9"])
                for eng, lo in (("pool", 0), ("dve", H2)):
                    sch.op(eng, lambda x, tg=tg, lo=lo: x.tensor_tensor(
                        out=A8[:, tg, lo:lo + H2], in0=A9[:, tg, lo:lo + H2],
                        in1=bet_s[:, lo:lo + H2], op=ALU.add),
                        reads=["A9", "beta"], writes=["A8"])
                for eng, lo in (("pool", 0), ("dve", H2)):
                    sch.op(eng, lambda x, tg=tg, lo=lo: x.tensor_tensor(
                        out=A9[:, tg, lo:lo + H2], in0=A8[:, tg, lo:lo + H2],
                        in1=A7[:, tg, lo:lo + H2], op=ALU.mult),
                        reads=["A8", "A7"], writes=["A9"])

            for tg in range(2):
                for fb in range(8):
                    sch.op("pe", lambda t, tg=tg, fb=fb:
                           t.transpose(out=PT[tg][:, fb, :],
                                       in_=A9[:, tg, fb * 128:(fb + 1) * 128],
                                       identity=ident_s[:]),
                           reads=["A9", "ident"], writes=[tkey[tg]], inc=(fb == 7))
            for tg in range(2):
                sch.op("dve", lambda v, tg=tg:
                       v.tensor_copy(out=A2[:, :, tg * 128:(tg + 1) * 128], in_=PT[tg][:]),
                       reads=[tkey[tg]], writes=["A2"])

            if have1:
                attn_front(i + 1, 0)

            sl3 = new_slot()
            for tg in range(2):
                for half in range(2):
                    bank = PF[(tg * 2 + half) % 2]
                    bkey = fkey[(tg * 2 + half) % 2]
                    for fc2 in range(8):
                        sch.op("pe", lambda t, tg=tg, h=half, fc=fc2, b=bank:
                               t.matmul(out=b[:],
                                        lhsT=A2[:, fc, tg * 128:(tg + 1) * 128],
                                        rhs=wo[:, fc, h * 512:(h + 1) * 512],
                                        start=(fc == 0), stop=(fc == 7)),
                               reads=["A2", "wo"], writes=[bkey], inc=(fc2 == 7))
                    sch.op("act", lambda s, tg=tg, h=half, b=bank:
                           s.copy(out=R2b[:, tg, h * 512:(h + 1) * 512], in_=b[:]),
                           reads=[bkey], writes=["R2b"])
                    for q in (2 * half, 2 * half + 1):
                        ln_stats(sl3, tg, q, R2b[:, tg, q * 256:(q + 1) * 256], "R2b")

            ln_finish(sl3)

            if have1:
                stage_a2(i + 1)

            sl4 = new_slot()

            def lnr1_out(v, tg, m, r):
                return v.tensor_scalar(out=A8[:, tg, :], in0=R2b[:, tg, :],
                                       scalar1=m, scalar2=r, op0=ALU.subtract, op1=ALU.mult)
            for tg in range(2):
                ln_apply(sl3, tg, lnr1_out, ["R2b"], ["A8", "r1a0"] if tg == 0 else "A8")
                for eng, lo, w in (("pool", 0, E // 4), ("dve", E // 4, 3 * E // 4)):
                    sch.op(eng, lambda x, tg=tg, lo=lo, w=w: x.tensor_tensor(
                        out=xsl[:, tg, lo:lo + w], in0=xsl[:, tg, lo:lo + w],
                        in1=A8[:, tg, lo:lo + w], op=ALU.add),
                        reads=[xkey, "A8"], writes=[xkey])
                for q in range(4):
                    ln_stats(sl4, tg, q, xsl[:, tg, q * 256:(q + 1) * 256], xkey)
            if have2:
                sch.op("syncq", lambda s, j=i + 2:
                       s.dma_start(out=xs[j % 3][:], in_=x_tiles[j]), writes=[f"x{(i + 2) % 3}"])

            ln_finish(sl4)

            def ln3_out(v, tg, m, r):
                return v.tensor_scalar(out=A1[:, tg, :], in0=xsl[:, tg, :],
                                       scalar1=m, scalar2=r, op0=ALU.subtract, op1=ALU.mult)
            for tg in range(2):
                ln_apply(sl4, tg, ln3_out, [xkey], "A1")

            if have1:
                sl2_slots[i + 1] = new_slot()
                attn_back(i + 1, 0, sl2_slots[i + 1], tail=False)
            sl5 = new_slot()

            for tg in range(2):
                for fb in range(8):
                    sch.op("pe", lambda t, tg=tg, fb=fb:
                           t.transpose(out=PT[tg][:, fb, :],
                                       in_=A1[:, tg, fb * 128:(fb + 1) * 128],
                                       identity=ident_s[:]),
                           reads=["A1", "ident"], writes=[tkey[tg]], inc=(fb == 7))
            for tg in range(2):
                sch.op("dve", lambda v, tg=tg:
                       v.tensor_copy(out=ln3fm[:, :, 1 + tg * 128:1 + (tg + 1) * 128], in_=PT[tg][:]),
                       reads=[tkey[tg]], writes=["ln3"])

            if have1:
                attn_front(i + 1, 1, defer=True)

            d8 = A2[:]
            xr = A8[:].rearrange("p g (c t) -> p (g c) t", c=4)
            xk = A9[:].rearrange("p g (c t) -> p (g c) t", c=4)
            cur = ln3fm[:, :, 1:T + 1]
            prev = ln3fm[:, :, 0:T]
            murb = mur_s[:].unsqueeze(2).broadcast_to((128, 8, T))
            k_r = A1[:].rearrange("p g (c t) -> p (g c) t", c=4)

            def k2_mm(dp, dc, fcs, inc_last=False):
                bank = PF[dp % 2]
                bkey = fkey[dp % 2]
                last = fcs[-1]
                for fc2 in fcs:
                    sch.op("pe", lambda t, dc=dc, fc=fc2, b=bank:
                           t.matmul(out=b[:, (dc % 2) * 256:(dc % 2) * 256 + 256],
                                    lhsT=wkT[:, fc, dc * 128:(dc + 1) * 128],
                                    rhs=xk[:, fc, :],
                                    start=(fc == 0), stop=(fc == 7)),
                           reads=["wkT", "A9"], writes=[bkey],
                           inc=(fc2 == 7 and dc % 2 == 1) or (inc_last and fc2 == last))

            def k2_relu(dp):
                bank = PF[dp % 2]
                bkey = fkey[dp % 2]
                sch.op("act", lambda s, dp=dp, b=bank:
                       s.activation(out=k_r[:, 2 * dp:2 * dp + 2, :],
                                    in_=b[:].rearrange("p (a b) -> p a b", a=2), func=AF.Relu),
                       reads=[bkey], writes=["A1"])

            for hlf in range(2):
                fsl = slice(4 * hlf, 4 * hlf + 4)
                mukb = muk_s[:, fsl].unsqueeze(2).broadcast_to((128, 4, T))
                sch.op("dve", lambda v, fsl=fsl: v.tensor_tensor(
                    out=d8[:, fsl], in0=cur[:, fsl], in1=prev[:, fsl], op=ALU.subtract),
                    reads=["ln3"], writes=["A2"])
                sch.op("dve", lambda v, fsl=fsl, mukb=mukb: v.tensor_tensor(
                    out=xk[:, fsl], in0=d8[:, fsl], in1=mukb, op=ALU.mult),
                    reads=["A2", "muk"], writes=["A9"])
                sch.op("dve", lambda v, fsl=fsl: v.tensor_tensor(
                    out=xk[:, fsl], in0=xk[:, fsl], in1=prev[:, fsl], op=ALU.add),
                    reads=["A9", "ln3"], writes=["A9"])
                if have1:
                    front_mask(2 + hlf)
                if hlf == 0:
                    k2_mm(0, 0, list(range(4)), inc_last=True)
            k2_mm(0, 0, range(4, 8))
            k2_mm(0, 1, range(8))
            k2_relu(0)

            sch.op("dve", lambda v: v.tensor_tensor(out=xr, in0=d8, in1=murb, op=ALU.mult),
                   reads=["A2", "mur"], writes=["A8"])
            sch.op("dve", lambda v: v.tensor_tensor(out=xr, in0=xr, in1=prev, op=ALU.add),
                   reads=["A8", "ln3"], writes=["A8"])
            if have1:
                front_mask(0)
                front_mask(1)
                front_qk_dve(1)
            sch.op("pool", lambda g: g.tensor_copy(out=ln3fm[:, :, 0:1], in_=ln3fm[:, :, T:T + 1]),
                   reads=["ln3"], writes=["ln3"], small=True)

            for dp in (1, 2, 3):
                for dc in (2 * dp, 2 * dp + 1):
                    k2_mm(dp, dc, range(8))
                k2_relu(dp)
            sch.op("pool", lambda g: g.tensor_tensor(
                out=A2[:].rearrange("p a b -> p (a b)"),
                in0=k_r.rearrange("p a b -> p (a b)"),
                in1=k_r.rearrange("p a b -> p (a b)"), op=ALU.mult),
                reads=["A1"], writes=["A2"])

            for tg in range(2):
                for half in range(2):
                    bank = PF[2 + (tg * 2 + half) % 2]
                    bkey = fkey[2 + (tg * 2 + half) % 2]
                    for fc2 in range(8):
                        sch.op("pe", lambda t, tg=tg, h=half, fc=fc2, b=bank:
                               t.matmul(out=b[:],
                                        lhsT=xr[:, fc, tg * 128:(tg + 1) * 128],
                                        rhs=wrT[:, fc, h * 512:(h + 1) * 512],
                                        start=(fc == 0), stop=(fc == 7)),
                               reads=["A8", "wrT"], writes=[bkey], inc=(fc2 == 7))
                    sch.op("act", lambda s, tg=tg, h=half, b=bank:
                           s.activation(out=A9[:, tg, h * 512:(h + 1) * 512], in_=b[:],
                                        func=AF.Tanh, scale=0.5),
                           reads=[bkey], writes=["A9"])
                    sch.op("act", lambda s, tg=tg, h=half:
                           s.activation(out=A9[:, tg, h * 512:(h + 1) * 512],
                                        in_=A9[:, tg, h * 512:(h + 1) * 512],
                                        func=AF.Identity, scale=0.5, bias=halfb[:]),
                           reads=["A9", "half"], writes=["A9"])

            if have1:
                attn_back_tail(0, sl2_slots[i + 1])
            if have2:
                ln1_prestats(i + 2)
                ln1_apply(i + 2)
            if have1:
                attn_back(i + 1, 1, sl2_slots[i + 1])

            y2banks = (4, 5, 2, 3)
            for tg in range(2):
                for half in range(2):
                    bank = PF[y2banks[tg * 2 + half]]
                    bkey = fkey[y2banks[tg * 2 + half]]
                    for fc2 in range(8):
                        sch.op("pe", lambda t, tg=tg, h=half, fc=fc2, b=bank:
                               t.matmul(out=b[:],
                                        lhsT=A2[:, fc, tg * 128:(tg + 1) * 128],
                                        rhs=wvT[:, fc, h * 512:(h + 1) * 512],
                                        start=(fc == 0), stop=(fc == 7)),
                               reads=["A2", "wvT"], writes=[bkey], inc=(fc2 == 7))
                    sch.op("dve", lambda v, tg=tg, h=half, b=bank:
                           v.tensor_tensor(out=R2b[:, tg, h * 512:(h + 1) * 512], in0=b[:],
                                           in1=A9[:, tg, h * 512:(h + 1) * 512], op=ALU.mult),
                           reads=[bkey, "A9"], writes=["R2b"])
                    for q in (2 * half, 2 * half + 1):
                        ln_stats(sl5, tg, q, R2b[:, tg, q * 256:(q + 1) * 256], "R2b")

            if have2:
                stage_a_ln(i + 2, do_apply=False)
                stage_a(i + 2)

            if have1 and sl2_slots[i + 1] + 1 == sl5:
                ln_finish2(sl2_slots[i + 1], sl5)
            else:
                if have1:
                    ln_finish(sl2_slots[i + 1])
                ln_finish(sl5)

            def lnr2_out(v, tg, m, r):
                return v.tensor_scalar(out=A8[:, tg, :], in0=R2b[:, tg, :],
                                       scalar1=m, scalar2=r, op0=ALU.subtract, op1=ALU.mult)
            for tg in range(2):
                ln_apply(sl5, tg, lnr2_out, ["R2b"], "A8")
                for eng, lo, w in (("pool", 0, E // 4), ("dve", E // 4, 3 * E // 4)):
                    sch.op(eng, lambda x, tg=tg, lo=lo, w=w: x.tensor_tensor(
                        out=xsl[:, tg, lo:lo + w], in0=xsl[:, tg, lo:lo + w],
                        in1=A8[:, tg, lo:lo + w], op=ALU.add),
                        reads=[xkey, "A8"], writes=[xkey])
            sch.op("syncq", lambda s, i=i: s.dma_start(out=out_tiles[i], in_=xsl[:]),
                   reads=[xkey], writes=[])

        ln1_prestats(0)
        stage_a_ln(0)
        stage_a(0)
        stage_a2(0)
        attn_front(0, 0)
        sl2_slots[0] = new_slot()
        attn_back(0, 0, sl2_slots[0])
        attn_front(0, 1)
        attn_back(0, 1, sl2_slots[0])
        if ntiles > 1:
            sch.op("syncq", lambda s: s.dma_start(out=xs[1][:], in_=x_tiles[1]),
                   writes=["x1"])
            ln1_prestats(1)
            stage_a_ln(1)
            stage_a(1)
        for i in range(ntiles):
            tile(i)

        final = sch.cnt["syncq"]
        sch.raw("sync", lambda s: s.wait_ge(s_syncq, final))

        @block.tensor
        def _(t):
            for fn in sch.progs["pe"]:
                fn(t)

        @block.vector
        def _(v):
            for fn in sch.progs["dve"]:
                fn(v)

        @block.scalar
        def _(s):
            for fn in sch.progs["act"]:
                fn(s)

        @block.gpsimd
        def _(g):
            for fn in sch.progs["pool"]:
                fn(g)

        @block.sync
        def _(s):
            for fn in sch.progs["sync"]:
                fn(s)

    return nc



def kernel(L0_attn_x, L0_attn_Wq, L0_attn_Wk, L0_attn_Wv, L0_attn_Wo,
           L0_attn_gamma, L0_attn_beta, L0_attn_Wg1, L0_attn_Wg2,
           L0_ffn_W_r, L0_ffn_W_k, L0_ffn_W_v, L0_ffn_mu_r, L0_ffn_mu_k):
    x = np.ascontiguousarray(np.asarray(L0_attn_x, dtype=np.float32))
    B = x.shape[0]
    tabs = host_tables()
    gamma_b = np.ascontiguousarray(np.broadcast_to(
        np.asarray(L0_attn_gamma, np.float32)[None, :], (128, E)))
    beta_b = np.ascontiguousarray(np.broadcast_to(
        np.asarray(L0_attn_beta, np.float32)[None, :], (128, E)))
    mur = np.ascontiguousarray(np.asarray(L0_ffn_mu_r, np.float32).reshape(8, 128).T)
    muk = np.ascontiguousarray(np.asarray(L0_ffn_mu_k, np.float32).reshape(8, 128).T)

    shared = {
        "Wq": np.asarray(L0_attn_Wq, np.float32), "Wk": np.asarray(L0_attn_Wk, np.float32),
        "Wv": np.asarray(L0_attn_Wv, np.float32), "Wo": np.asarray(L0_attn_Wo, np.float32),
        "Wg1": np.asarray(L0_attn_Wg1, np.float32),
        "Wg2": np.concatenate([np.asarray(L0_attn_Wg2, np.float32),
                               np.zeros((64, E), np.float32)], axis=0),
        "W_r": np.asarray(L0_ffn_W_r, np.float32), "W_k2": np.asarray(L0_ffn_W_k, np.float32),
        "W_v2": np.asarray(L0_ffn_W_v, np.float32),
        "gammab": gamma_b, "betab": beta_b, "mur": mur, "muk": muk,
        "maskT": tabs["maskT"], "lamq": tabs["lamq"], "lamk": tabs["lamk"],
        "lamc_bc": tabs["lamc_bc"], "ident": tabs["ident"],
    }
    nc = build_program()
    in_maps = [dict(shared, x=np.ascontiguousarray(x[b])) for b in range(B)]
    res = run_bass_kernel_spmd(nc, in_maps, core_ids=list(range(B)))
    out = np.stack([res.results[b]["out"] for b in range(B)], axis=0)
    return out


# version 53 (final)
# speedup vs baseline: 1.2098x; 1.2098x over previous
import numpy as np
import ml_dtypes
import concourse.bass as bass
import concourse.mybir as mybir
from concourse.bass_utils import run_bass_kernel_spmd

F32 = mybir.dt.float32
BF16 = mybir.dt.bfloat16
I32 = mybir.dt.int32
ALU = mybir.AluOpType
AF = mybir.ActivationFunctionType
AX = mybir.AxisListType

E, H, D, CH = 1024, 16, 64, 128
T = 256
NTOK = 8192
NCORES = 8
EPS = 1.0e-5
RSQRT_MAGIC = 0x5F3759DF



def host_tables():
    s = np.array([2.0 ** (-(i + 1) / 2.0) for i in range(H)], dtype=np.float32)
    s = s * np.float32(1.0 + 1e-5)
    idx = np.arange(CH, dtype=np.float32)
    diff = idx[:, None] - idx[None, :]
    mask = np.where(diff >= 0, np.exp(-s[:, None, None] * diff), 0.0).astype(np.float32)
    lam_q = np.exp(-s[:, None] * (idx + 1.0)).astype(np.float32)
    lam_k = np.exp(-s[:, None] * (CH - 1.0 - idx)).astype(np.float32)
    lam_c = np.exp(-s * CH).astype(np.float32)

    bf = ml_dtypes.bfloat16
    HEAD_ORDER = [0, 2, 4, 6, 1, 3, 5, 7, 8, 10, 12, 14, 9, 11, 13, 15]
    maskT = np.ascontiguousarray(mask.transpose(2, 0, 1)[:, HEAD_ORDER, :]).astype(bf)
    p = np.arange(128)
    fc = np.arange(8)
    hh = 2 * fc[None, :] + (p[:, None] >= 64)
    lamq = lam_q[hh][:, :, :].astype(bf)
    dd = np.arange(128)
    hh2 = 2 * fc[:, None] + (dd[None, :] >= 64)
    lamk = np.ascontiguousarray(lam_k[hh2].transpose(2, 0, 1)).astype(bf)
    hp = np.arange(8)
    hh3 = 2 * hp[None, :] + (p[:, None] >= 64)
    lamc_bc = lam_c[hh3].astype(np.float32)
    ident = np.eye(128, dtype=bf)
    return {"maskT": maskT, "lamq": lamq, "lamk": lamk, "lamc_bc": lamc_bc, "ident": ident}



class Sched:
    COMPUTE = ("pe", "act", "dve", "pool")
    QUEUES = ("syncq", "poolq", "actq")
    STREAM_OF = {"pe": "pe", "act": "act", "dve": "dve", "pool": "pool",
                 "syncq": "sync", "poolq": "pool", "actq": "act"}

    def __init__(self):
        self.progs = {s: [] for s in ("pe", "act", "dve", "pool", "sync")}
        self.cnt = {e: 0 for e in list(self.COMPUTE) + list(self.QUEUES)}
        self.writer = {}
        self.readers = {}
        self.waited = {}
        self.sems = {}
        self.nwaits = 0

    def _want(self, wants, waiter, src, val, src_small=False):
        if val <= 0:
            return
        if self.STREAM_OF[src] == self.STREAM_OF[waiter]:
            if not src_small:
                return
        if self.waited.get((waiter, src), 0) >= val:
            return
        wants[src] = max(wants.get(src, 0), val)

    def _flush_waits(self, waiter, wants):
        for src, val in wants.items():
            self.waited[(waiter, src)] = val
            self.progs[self.STREAM_OF[waiter]].append(
                lambda eng, s=src, v=val: eng.wait_ge(self.sems[s], v))
            self.nwaits += 1

    def op(self, eng, fn, reads=(), writes=(), inc=True, small=False):
        wants = {}
        for b in reads:
            w = self.writer.get(b)
            if w:
                self._want(wants, eng, w[0], w[1], w[2] if len(w) > 2 else False)
        for b in writes:
            w = self.writer.get(b)
            if w:
                self._want(wants, eng, w[0], w[1], w[2] if len(w) > 2 else False)
            for r_eng, r_cnt in self.readers.get(b, {}).items():
                self._want(wants, eng, r_eng, r_cnt)
        self._flush_waits(eng, wants)
        is_dma = eng in self.QUEUES
        step = 16 if is_dma else (1 if inc else 0)
        self.cnt[eng] += step
        val = self.cnt[eng] if step else self.cnt[eng] + 1
        if step:
            self.progs[self.STREAM_OF[eng]].append(
                lambda h, fn=fn, e=eng, s=step: fn(h).then_inc(self.sems[e], s))
        else:
            self.progs[self.STREAM_OF[eng]].append(lambda h, fn=fn: fn(h))
        for b in reads:
            self.readers.setdefault(b, {})
            prev = self.readers[b].get(eng, 0)
            self.readers[b][eng] = max(prev, val)
        if small:
            assert step == 1, "small ops must inc"
        for b in writes:
            self.writer[b] = (eng, val, small)
            self.readers[b] = {}
        return val

    def raw(self, stream, fn):
        self.progs[stream].append(fn)



def build_program(ntiles=NTOK // T, debug=False, stage_limit=99, skip=()):
    nc = bass.Bass("TRN2", target_bir_lowering=False, detect_race_conditions=False)
    ntok = ntiles * T

    x_h = nc.dram_tensor("x", [ntok, E], F32, kind="ExternalInput").ap()
    out_h = nc.dram_tensor("out", [ntok, E], F32, kind="ExternalOutput").ap()
    wq_h = nc.dram_tensor("Wq", [E, E], F32, kind="ExternalInput").ap()
    wk_h = nc.dram_tensor("Wk", [E, E], F32, kind="ExternalInput").ap()
    wv_h = nc.dram_tensor("Wv", [E, E], F32, kind="ExternalInput").ap()
    wo_h = nc.dram_tensor("Wo", [E, E], F32, kind="ExternalInput").ap()
    wg1_h = nc.dram_tensor("Wg1", [E, D], F32, kind="ExternalInput").ap()
    wg2_h = nc.dram_tensor("Wg2", [128, E], F32, kind="ExternalInput").ap()
    wr_h = nc.dram_tensor("W_r", [E, E], F32, kind="ExternalInput").ap()
    wk2_h = nc.dram_tensor("W_k2", [E, E], F32, kind="ExternalInput").ap()
    wv2_h = nc.dram_tensor("W_v2", [E, E], F32, kind="ExternalInput").ap()
    gamma_h = nc.dram_tensor("gammab", [128, E], F32, kind="ExternalInput").ap()
    beta_h = nc.dram_tensor("betab", [128, E], F32, kind="ExternalInput").ap()
    mur_h = nc.dram_tensor("mur", [128, 8], F32, kind="ExternalInput").ap()
    muk_h = nc.dram_tensor("muk", [128, 8], F32, kind="ExternalInput").ap()
    maskT_h = nc.dram_tensor("maskT", [128, 16, 128], BF16, kind="ExternalInput").ap()
    lamq_h = nc.dram_tensor("lamq", [128, 8, 128], BF16, kind="ExternalInput").ap()
    lamk_h = nc.dram_tensor("lamk", [128, 8, 128], BF16, kind="ExternalInput").ap()
    lamc_h = nc.dram_tensor("lamc_bc", [128, 8], F32, kind="ExternalInput").ap()
    ident_h = nc.dram_tensor("ident", [128, 128], BF16, kind="ExternalInput").ap()

    dbg = {}
    if debug:
        for nm, shape, dt in [
            ("xhat1_tm", [128, 2, E], BF16), ("xhat1_fm", [128, 8, T], BF16),
            ("q", [128, 8, T], BF16), ("k", [128, 8, T], BF16),
            ("v", [128, 2, E], BF16), ("gate", [128, 2, E], BF16),
            ("o", [128, 2, E], F32), ("og", [128, 2, E], BF16),
            ("y", [128, 2, E], F32), ("xnew", [128, 2, E], F32),
            ("ln3", [128, 8, T + 1], BF16), ("xr", [128, 8, T], BF16),
            ("xk", [128, 8, T], BF16), ("r", [128, 2, E], BF16),
            ("kact", [128, 8, T], BF16), ("y2f", [128, 2, E], F32),
            ("ktil0", [128, 8, 128], BF16), ("qhat0", [128, 8, 128], BF16),
            ("sm0", [128, 2, 4, 128], BF16), ("kv0", [128, 8, D], F32),
        ]:
            dbg[nm] = nc.dram_tensor("dbg_" + nm, shape, dt, kind="ExternalOutput").ap()

    x_tiles = x_h.rearrange("(n g p) e -> n p g e", p=128, g=2)
    out_tiles = out_h.rearrange("(n g p) e -> n p g e", p=128, g=2)

    sch = Sched()

    from contextlib import ExitStack
    with ExitStack() as _stack:
        def SB(name, shape, dt):
            return _stack.enter_context(nc.sbuf_tensor(name, shape, dt))

        def PS(name, shape, dt):
            return _stack.enter_context(nc.psum_tensor(name, shape, dt))

        def SEM(name):
            return _stack.enter_context(nc.semaphore(name))

        x0 = SB("x0", [128, 2, E], F32)
        x1 = SB("x1", [128, 2, E], F32)
        x2 = SB("x2", [128, 2, E], F32)
        OBF = SB("OBF", [128, 2, E], BF16)
        R2b = SB("R2b", [128, 2, E], BF16)
        kvb = SB("kv", [128, 8, D], F32)
        statsb = SB("stats", [128, 4, 2, 4, 6], F32)
        mvb = SB("mv", [128, 4, 2, 2], F32)
        ncolb = SB("ncol", [128, 4, 4, 2], F32)
        halfb = SB("halfb", [128, 1], F32)
        lamc_s = SB("lamc_s", [128, 8], F32)
        mur_s = SB("mur_s", [128, 8], F32)
        muk_s = SB("muk_s", [128, 8], F32)
        A1 = SB("A1", [128, 2, E], BF16)
        A2 = SB("A2", [128, 8, T], BF16)
        A3 = SB("A3", [128, 8, T], BF16)
        A4 = SB("A4", [128, 8, T], BF16)
        A5 = SB("A5", [128, 2, E], BF16)
        A6 = SB("A6", [128, 8, CH], BF16)
        A7 = SB("A7", [128, 2, E], BF16)
        A8 = SB("A8", [128, 2, E], BF16)
        A9 = SB("A9", [128, 2, E], BF16)
        ktil = SB("ktil", [128, 8, 128], BF16)
        smb = SB("sm", [128, 2, 2, 4, 128], BF16)
        kvbf = SB("kvbf", [128, 8, D], BF16)
        g64b = SB("g64", [128, T], BF16)
        wq = SB("wq", [128, 8, E], BF16)
        wk = SB("wk", [128, 8, E], BF16)
        wv = SB("wv", [128, 8, E], BF16)
        wo = SB("wo", [128, 8, E], BF16)
        wrT = SB("wrT", [128, 8, E], BF16)
        wkT = SB("wkT", [128, 8, E], BF16)
        wvT = SB("wvT", [128, 8, E], BF16)
        wg1 = SB("wg1", [128, 8, D], BF16)
        wg2 = SB("wg2", [128, E], BF16)
        gam_s = SB("gammab_s", [128, E], BF16)
        bet_s = SB("betab_s", [128, E], BF16)
        maskT_s = SB("maskT_s", [128, 16, 128], BF16)
        lamq_s = SB("lamq_s", [128, 8, 128], BF16)
        lamk_s = SB("lamk_s", [128, 8, 128], BF16)
        ident_s = SB("ident_s", [128, 128], BF16)
        ln3fm = SB("ln3fm", [128, 8, T + 1], BF16)
        PF0 = PS("PF0", [128, 512], F32)
        PF1 = PS("PF1", [128, 512], F32)
        PF2 = PS("PF2", [128, 512], F32)
        PF3 = PS("PF3", [128, 512], F32)
        PF4 = PS("PF4", [128, 512], F32)
        PF5 = PS("PF5", [128, 512], F32)
        PT0 = PS("PT0", [128, 8, 128], BF16)
        PT1 = PS("PT1", [128, 8, 128], BF16)
        s_pe = SEM("s_pe")
        s_act = SEM("s_act")
        s_dve = SEM("s_dve")
        s_pool = SEM("s_pool")
        s_syncq = SEM("s_syncq")
        s_poolq = SEM("s_poolq")
        s_actq = SEM("s_actq")
        block = _stack.enter_context(nc.Block())

        sch.sems = {"pe": s_pe, "act": s_act, "dve": s_dve, "pool": s_pool,
                    "syncq": s_syncq, "poolq": s_poolq, "actq": s_actq}
        def tap(nm, buf_ap, key):
            if debug and nm in dbg:
                sch.op("syncq", lambda s, d=dbg[nm], b=buf_ap: s.dma_start(out=d[:], in_=b),
                       reads=[key], writes=[])
                del dbg[nm]

        xs = [x0, x1, x2]
        PF = [PF0, PF1, PF2, PF3, PF4, PF5]
        PT = [PT0, PT1]
        fkey = ["PF0", "PF1", "PF2", "PF3", "PF4", "PF5"]
        tkey = ["PT0", "PT1"]

        for name, dst, src in [("maskT", maskT_s, maskT_h), ("lamq", lamq_s, lamq_h),
                               ("lamk", lamk_s, lamk_h), ("ident", ident_s, ident_h),
                               ("lamc", lamc_s, lamc_h), ("mur", mur_s, mur_h),
                               ("muk", muk_s, muk_h)]:
            sch.op("poolq", lambda g, d=dst, s=src: g.dma_start(out=d[:], in_=s[:]),
                   writes=[name])
        sch.op("pool", lambda g: g.memset(kvb[:], 0.0), writes=["kv"])
        sch.op("pool", lambda g: g.memset(halfb[:], 0.5), writes=["half"])
        sch.op("pool", lambda g: g.memset(kvbf[:], 0.0), writes=["kvbf"])
        sch.op("pool", lambda g: g.memset(ln3fm[:, :, 0:1], 0.0), writes=["ln3"])

        sch.op("syncq", lambda s: s.dma_start(out=x0[:], in_=x_tiles[0]), writes=["x0"])

        stage = x1[:].rearrange("p g e -> p (g e)")
        for name, dsts, srcs in [("gamma", gam_s, gamma_h), ("beta", bet_s, beta_h)]:
            sch.op("syncq", lambda s2, d=stage, s=srcs: s2.dma_start(out=d[:, 0:E], in_=s[:]),
                   reads=(), writes=["x1"])
            sch.op("pool", lambda g, d=dsts, s=stage: g.tensor_copy(out=d[:], in_=s[:, 0:E]),
                   reads=["x1"], writes=[name])

        stage2 = x2[:].rearrange("p g e -> p (g e)")
        wph = [0]

        def _wstage():
            p = wph[0] % 2
            wph[0] += 1
            return (stage, "x1", "syncq") if p == 0 else (stage2, "x2", "actq")

        def load_weight(dst, src_h, key):
            for ic2 in range(4):
                stg, skey, q = _wstage()
                half = src_h[ic2 * 256:(ic2 + 1) * 256, :].rearrange("(a p) e -> p a e", p=128)
                sch.op(q, lambda s, d=stg, hh=half:
                       s.dma_start(out=d[:].rearrange("p (a e) -> p a e", a=2), in_=hh),
                       writes=[skey])
                for eng, lo in (("pool", 0), ("dve", E)):
                    sch.op(eng, lambda g, d=dst, i=ic2, stg=stg, lo=lo:
                           g.tensor_copy(out=d[:, 2 * i + lo // E, :],
                                         in_=stg[:, lo:lo + E]),
                           reads=[skey], writes=[key])

        load_weight(wq, wq_h, "wq")
        load_weight(wk, wk_h, "wk")
        load_weight(wv, wv_h, "wv")
        sch.op("syncq", lambda s: s.dma_start(
            out=stage[:, 0:512].rearrange("p (a e) -> p a e", a=8),
            in_=wg1_h[:].rearrange("(a p) e -> p a e", p=128)), writes=["x1"])
        sch.op("pool", lambda g: g.tensor_copy(
            out=wg1[:], in_=stage[:, 0:512].rearrange("p (a e) -> p a e", a=8)),
            reads=["x1"], writes=["wg1"])
        sch.op("syncq", lambda s: s.dma_start(out=stage[:, 0:E], in_=wg2_h[:]),
               writes=["x1"])
        sch.op("pool", lambda g: g.tensor_copy(out=wg2[:], in_=stage[:, 0:E]),
               reads=["x1"], writes=["wg2"])
        load_weight(wo, wo_h, "wo")

        cast1 = A5[:].rearrange("p g e -> p (g e)")
        cast2 = A3[:].rearrange("p g e -> p (g e)")

        def load_weight_T(dstT, src_h, key):
            for ic2 in range(4):
                stg, skey, q = _wstage()
                cast, ckey = (cast1, "A5") if skey == "x1" else (cast2, "A3")
                half = src_h[ic2 * 256:(ic2 + 1) * 256, :].rearrange("(a p) e -> p a e", p=128)
                sch.op(q, lambda s, d=stg, hh=half:
                       s.dma_start(out=d[:].rearrange("p (a e) -> p a e", a=2), in_=hh),
                       writes=[skey])
                for eng, lo in (("pool", 0), ("dve", E)):
                    sch.op(eng, lambda g, stg=stg, cast=cast, lo=lo:
                           g.tensor_copy(out=cast[:, lo:lo + E], in_=stg[:, lo:lo + E]),
                           reads=[skey], writes=[ckey])
                for a in range(2):
                    dcc = ic2 * 2 + a
                    pt = PT[dcc % 2]
                    ptk = tkey[dcc % 2]
                    for cc in range(8):
                        last = (cc == 7)
                        sch.op("pe", lambda t, a=a, cc=cc, cast=cast, pt=pt:
                               t.transpose(out=pt[:, cc, :],
                                           in_=cast[:, a * E + cc * 128:a * E + (cc + 1) * 128],
                                           identity=ident_s[:]),
                               reads=[ckey, "ident"], writes=[ptk], inc=last)
                    sch.op("dve", lambda v, d=dstT, dcc=dcc, pt=pt:
                           v.tensor_copy(out=d[:, :, dcc * 128:(dcc + 1) * 128],
                                         in_=pt[:]),
                           reads=[ptk], writes=[key])

        load_weight_T(wrT, wr_h, "wrT")
        load_weight_T(wkT, wk2_h, "wkT")
        load_weight_T(wvT, wv2_h, "wvT")

        def ln_stats(slot, tg, q, src2d, src_key):
            sch.op("dve", lambda v, s=slot, tg=tg, q=q, src=src2d:
                   v.bn_stats(out=statsb[:, s, tg, q, :], in_=src),
                   reads=[src_key], writes=[("st", slot, tg, q)], small=True)

        def ln_finish(slot):
            st = statsb[:, slot]
            mv = mvb[:, slot]
            ncol = ncolb[:, slot]
            u, y, t2 = ncol[:, 0], ncol[:, 1], ncol[:, 2]
            for tg in range(2):
                sch.op("dve", lambda v, tg=tg, st=st, mv=mv: v.bn_aggr(out=mv[:, tg, :], in_=st[:, tg]),
                       reads=[("st", slot, tg, q) for q in range(4)],
                       writes=[("mv", slot)], small=True)
            mvv = mvb[:, slot, :, 1]
            sch.op("dve", lambda v, u=u, mvv=mvv: v.tensor_single_scalar(out=u, in_=mvv, scalar=EPS, op=ALU.add),
                   reads=[("mv", slot)], writes=[("u", slot)], small=True)
            sch.op("dve", lambda v, u=u, y=y: v.tensor_single_scalar(
                out=y.bitcast(I32), in_=u.bitcast(I32), scalar=1, op=ALU.logical_shift_right),
                reads=[("u", slot)], writes=[("y", slot)], small=True)
            sch.op("dve", lambda v, y=y: v.tensor_scalar(
                out=y.bitcast(I32), in0=y.bitcast(I32), scalar1=-1, scalar2=RSQRT_MAGIC,
                op0=ALU.mult, op1=ALU.add),
                reads=[("y", slot)], writes=[("y", slot)], small=True)
            for _ in range(1):
                sch.op("dve", lambda v, y=y, t2=t2: v.tensor_mul(out=t2, in0=y, in1=y),
                       reads=[("y", slot)], writes=[("t2", slot)], small=True)
                sch.op("dve", lambda v, u=u, t2=t2: v.tensor_mul(out=t2, in0=t2, in1=u),
                       reads=[("t2", slot), ("u", slot)], writes=[("t2", slot)], small=True)
                sch.op("dve", lambda v, t2=t2: v.tensor_scalar(
                    out=t2, in0=t2, scalar1=-0.5, scalar2=1.5, op0=ALU.mult, op1=ALU.add),
                    reads=[("t2", slot)], writes=[("t2", slot)], small=True)
                sch.op("dve", lambda v, y=y, t2=t2: v.tensor_mul(out=y, in0=y, in1=t2),
                       reads=[("y", slot), ("t2", slot)], writes=[("y", slot)], small=True)

        def ln_finish2(sa, sb):
            for slot in (sa, sb):
                st = statsb[:, slot]
                mv = mvb[:, slot]
                for tg in range(2):
                    sch.op("dve", lambda v, tg=tg, st=st, mv=mv: v.bn_aggr(out=mv[:, tg, :], in_=st[:, tg]),
                           reads=[("st", slot, tg, q) for q in range(4)],
                           writes=[("mv", slot)], small=True)
            u = ncolb[:, sa:sa + 2, 0, :]
            y = ncolb[:, sa:sa + 2, 1, :]
            t2 = ncolb[:, sa:sa + 2, 2, :]
            mvv = mvb[:, sa:sa + 2, :, 1]
            rd = [("mv", sa), ("mv", sb)]
            wU = [("u", sa), ("u", sb)]
            wY = [("y", sa), ("y", sb)]
            wT = [("t2", sa), ("t2", sb)]
            sch.op("dve", lambda v, u=u, mvv=mvv: v.tensor_single_scalar(out=u, in_=mvv, scalar=EPS, op=ALU.add),
                   reads=rd, writes=wU, small=True)
            sch.op("dve", lambda v, u=u, y=y: v.tensor_single_scalar(
                out=y.bitcast(I32), in_=u.bitcast(I32), scalar=1, op=ALU.logical_shift_right),
                reads=wU, writes=wY, small=True)
            sch.op("dve", lambda v, y=y: v.tensor_scalar(
                out=y.bitcast(I32), in0=y.bitcast(I32), scalar1=-1, scalar2=RSQRT_MAGIC,
                op0=ALU.mult, op1=ALU.add),
                reads=wY, writes=wY, small=True)
            sch.op("dve", lambda v, y=y, t2=t2: v.tensor_mul(out=t2, in0=y, in1=y),
                   reads=wY, writes=wT, small=True)
            sch.op("dve", lambda v, u=u, t2=t2: v.tensor_mul(out=t2, in0=t2, in1=u),
                   reads=wT + wU, writes=wT, small=True)
            sch.op("dve", lambda v, t2=t2: v.tensor_scalar(
                out=t2, in0=t2, scalar1=-0.5, scalar2=1.5, op0=ALU.mult, op1=ALU.add),
                reads=wT, writes=wT, small=True)
            sch.op("dve", lambda v, y=y, t2=t2: v.tensor_mul(out=y, in0=y, in1=t2),
                   reads=wY + wT, writes=wY, small=True)

        def ln_apply(slot, tg, out_fn, reads, out_key):
            m_ap = mvb[:, slot, tg, 0:1]
            r_ap = ncolb[:, slot, 1, tg:tg + 1]
            ok = out_key if isinstance(out_key, list) else [out_key]
            sch.op("dve", lambda v, tg=tg, m=m_ap, r=r_ap: out_fn(v, tg, m, r),
                   reads=list(reads) + [("mv", slot), ("y", slot)], writes=ok)

        ln_counter = [0]
        ln1_slots = {}
        sl2_slots = {}

        def new_slot():
            s = ln_counter[0] % 4
            ln_counter[0] += 1
            return s

        def ln1_prestats(j):
            xt = xs[j % 3]
            xkey = f"x{j % 3}"
            sl = new_slot()
            ln1_slots[j] = sl
            for tg in range(2):
                for q in range(4):
                    ln_stats(sl, tg, q, xt[:, tg, q * 256:(q + 1) * 256], xkey)
            ln_finish(sl)

        def ln1_apply(j):
            xt = xs[j % 3]
            xkey = f"x{j % 3}"
            sl = ln1_slots.pop(j)

            def ln1_out(v, tg, m, r):
                return v.tensor_scalar(out=A1[:, tg, :], in0=xt[:, tg, :],
                                       scalar1=m, scalar2=r, op0=ALU.subtract, op1=ALU.mult)
            for tg in range(2):
                ln_apply(sl, tg, ln1_out, [xkey], ["A1", ("A1t", tg)])

        def stage_a_ln(j, do_apply=True):
            if do_apply:
                ln1_apply(j)
            for tg in range(2):
                for fb in range(8):
                    sch.op("pe", lambda t, tg=tg, fb=fb:
                           t.transpose(out=PT[tg][:, fb, :],
                                       in_=A1[:, tg, fb * 128:(fb + 1) * 128],
                                       identity=ident_s[:]),
                           reads=["A1", "ident"], writes=[tkey[tg]], inc=(fb == 7))
            for tg in range(2):
                sch.op("dve", lambda v, tg=tg:
                       v.tensor_copy(out=A2[:, :, tg * 128:(tg + 1) * 128], in_=PT[tg][:]),
                       reads=[tkey[tg]], writes=["A2"])

        def stage_a(j):
            def proj_fm(wgt, wkeyn, dst, dkey, act):
                for dp in range(4):
                    bank = PF[dp % 2]
                    bkey = fkey[dp % 2]
                    for dc in (2 * dp, 2 * dp + 1):
                        for fc2 in range(8):
                            sch.op("pe", lambda t, dc=dc, fc=fc2, b=bank:
                                   t.matmul(out=b[:, (dc % 2) * 256:(dc % 2) * 256 + 256],
                                            lhsT=wgt[:, fc, dc * 128:(dc + 1) * 128],
                                            rhs=A2[:, fc, :],
                                            start=(fc == 0), stop=(fc == 7)),
                                   reads=[wkeyn, "A2"], writes=[bkey],
                                   inc=(fc2 == 7 and dc % 2 == 1))
                    sch.op("act", lambda s, dp=dp, b=bank, d=dst:
                           s.activation(out=d[:, 2 * dp:2 * dp + 2, :],
                                        in_=b[:].rearrange("p (a b) -> p a b", a=2), func=act),
                           reads=[bkey], writes=[dkey])

            proj_fm(wq, "wq", A3, "A3", AF.Silu)
            proj_fm(wk, "wk", A4, "A4", AF.Silu)
            for tg in range(2):
                for half in range(2):
                    bank = PF[2 + (tg * 2 + half) % 2]
                    bkey = fkey[2 + (tg * 2 + half) % 2]
                    for fc2 in range(8):
                        sch.op("pe", lambda t, tg=tg, h=half, fc=fc2, b=bank:
                               t.matmul(out=b[:],
                                        lhsT=A2[:, fc, tg * 128:(tg + 1) * 128],
                                        rhs=wv[:, fc, h * 512:(h + 1) * 512],
                                        start=(fc == 0), stop=(fc == 7)),
                               reads=["wv", "A2"], writes=[bkey], inc=(fc2 == 7))
                    sch.op("act", lambda s, tg=tg, h=half, b=bank:
                           s.activation(out=A5[:, tg, h * 512:(h + 1) * 512], in_=b[:], func=AF.Silu),
                           reads=[bkey], writes=["A5"])
            for fc2 in range(8):
                sch.op("pe", lambda t, fc=fc2:
                       t.matmul(out=PF4[0:64, 0:256], lhsT=wg1[:, fc, :], rhs=A2[:, fc, :],
                                start=(fc == 0), stop=(fc == 7)),
                       reads=["wg1", "A2"], writes=["PF4"], inc=(fc2 == 7))
            sch.op("act", lambda s: s.copy(out=g64b[0:64, :], in_=PF4[0:64, 0:256]),
                   reads=["PF4"], writes=["g64"])

        def stage_a2(j):
            for tg in range(2):
                for half in range(2):
                    bank = PF[(tg * 2 + half) % 2]
                    bkey = fkey[(tg * 2 + half) % 2]
                    sch.op("pe", lambda t, tg=tg, h=half, b=bank:
                           t.matmul(out=b[:], lhsT=g64b[0:64, tg * 128:(tg + 1) * 128],
                                    rhs=wg2[0:64, h * 512:(h + 1) * 512], start=True, stop=True),
                           reads=["g64", "wg2"], writes=[bkey])
                    sch.op("act", lambda s, tg=tg, h=half, b=bank:
                           s.activation(out=A7[:, tg, h * 512:(h + 1) * 512], in_=b[:],
                                        func=AF.Tanh, scale=0.5),
                           reads=[bkey], writes=["A7"])
                    sch.op("act", lambda s, tg=tg, h=half:
                           s.activation(out=A7[:, tg, h * 512:(h + 1) * 512],
                                        in_=A7[:, tg, h * 512:(h + 1) * 512],
                                        func=AF.Identity, scale=0.5, bias=halfb[:]),
                           reads=["A7", "half"], writes=["A7"])

        st_batches = [[8 * (b4 // 2) + 2 * ii + (b4 % 2) for ii in range(4)] for b4 in range(4)]
        st_banks = (2, 3, 0, 1)

        def front_mask(b4):
            sb = PF[st_banks[b4]]
            sbk = fkey[st_banks[b4]]
            sch.op("dve", lambda v, b4=b4, b=sb:
                   v.tensor_tensor(out=smb[:, b4 % 2, b4 // 2], in0=b[:].rearrange("p (a b) -> p a b", a=4),
                                   in1=maskT_s[:, b4 * 4:(b4 + 1) * 4, :], op=ALU.mult),
                   reads=[sbk, "maskT"], writes=[("sm", b4 % 2, b4 // 2)])

        def front_qk_dve(c):
            sch.op("dve", lambda v, c=c:
                   v.tensor_tensor(out=A6[:],
                                   in0=A3[:, :, c * 128:(c + 1) * 128],
                                   in1=lamq_s[:], op=ALU.mult),
                   reads=["A3", "lamq"], writes=["A6"])
            sch.op("dve", lambda v:
                   v.tensor_tensor(out=ktil[:], in0=PT0[:], in1=lamk_s[:], op=ALU.mult),
                   reads=["PT0", "lamk"], writes=["ktil"])

        def attn_front(i, c, defer=False):
            if defer is not True:
                sch.op("dve", lambda v, c=c:
                       v.tensor_tensor(out=A6[:],
                                       in0=A3[:, :, c * 128:(c + 1) * 128],
                                       in1=lamq_s[:], op=ALU.mult),
                       reads=["A3", "lamq"], writes=["A6"])
            for fb in range(8):
                sch.op("pe", lambda t, fb=fb, c=c:
                       t.transpose(out=PT0[:, fb, :], in_=A4[:, fb, c * 128:(c + 1) * 128],
                                   identity=ident_s[:]),
                       reads=["A4", "ident"], writes=["PT0"], inc=(fb == 7))
            for b4 in (2, 3, 0, 1):
                sb = PF[st_banks[b4]]
                sbk = fkey[st_banks[b4]]
                for hh in range(4):
                    h = st_batches[b4][hh]
                    fc2, po = h // 2, 64 * (h % 2)
                    sch.op("pe", lambda t, h=h, hh=hh, fc=fc2, po=po, b=sb, c=c:
                           t.matmul(out=b[:, hh * 128:(hh + 1) * 128],
                                    lhsT=A4[po:po + 64, fc, c * 128:(c + 1) * 128],
                                    rhs=A3[po:po + 64, fc, c * 128:(c + 1) * 128],
                                    start=True, stop=True),
                           reads=["A4", "A3"], writes=[sbk], inc=(hh == 3))
                if defer is not True:
                    front_mask(b4)
            if defer is False:
                sch.op("dve", lambda v:
                       v.tensor_tensor(out=ktil[:], in0=PT0[:], in1=lamk_s[:], op=ALU.mult),
                       reads=["PT0", "lamk"], writes=["ktil"])

        def attn_back(i, c, sl2, delay_key=None, tail=True):
            batches = [[8 * (b4 // 2) + 2 * ii + (b4 % 2) for ii in range(4)] for b4 in range(4)]
            first = [True]
            for b4 in range(4):
                for hh in range(4):
                    h = batches[b4][hh]
                    fc2, po = h // 2, 64 * (h % 2)
                    ob = PF[4 + h // 8]
                    obk = fkey[4 + h // 8]
                    extra = [delay_key] if (first[0] and delay_key) else []
                    first[0] = False
                    sch.op("pe", lambda t, h=h, hh=hh, b4=b4, b=ob, c=c:
                           t.matmul(out=b[:, (h % 8) * 64:(h % 8) * 64 + 64],
                                    lhsT=smb[:, b4 % 2, b4 // 2, hh, :],
                                    rhs=A5[:, c, h * 64:(h + 1) * 64],
                                    start=True, stop=False),
                           reads=[("sm", b4 % 2, b4 // 2), "A5"] + extra, writes=[obk], inc=False)
                    sch.op("pe", lambda t, h=h, fc=fc2, po=po, b=ob, c=c:
                           t.matmul(out=b[:, (h % 8) * 64:(h % 8) * 64 + 64],
                                    lhsT=A6[po:po + 64, fc, :],
                                    rhs=kvbf[po:po + 64, fc, :],
                                    start=False, stop=True),
                           reads=["A6", "kvbf"], writes=[obk],
                           inc=(h % 8 == 7))
            for h in range(16):
                fc2, po = h // 2, 64 * (h % 2)
                sch.op("pe", lambda t, h=h, fc=fc2, po=po, c=c:
                       t.matmul(out=PF0[po:po + 64, fc * 64:(fc + 1) * 64],
                                lhsT=ktil[:, fc, po:po + 64],
                                rhs=A5[:, c, h * 64:(h + 1) * 64],
                                start=True, stop=True),
                       reads=["ktil", "A5"], writes=["PF0"], inc=(h == 15))
            sch.op("pool", lambda g:
                   g.tensor_tensor(out=kvb[:], in0=kvb[:],
                                   in1=lamc_s[:].unsqueeze(2).broadcast_to((128, 8, D)), op=ALU.mult),
                   reads=["kv", "lamc"], writes=["kv"])
            sch.op("dve", lambda v:
                   v.tensor_tensor(out=kvb[:], in0=kvb[:],
                                   in1=PF0[:].rearrange("p (a b) -> p a b", a=8), op=ALU.add),
                   reads=["kv", "PF0"], writes=["kv"])
            if tail:
                attn_back_tail(c, sl2)

        def attn_back_tail(c, sl2):
            sch.op("pool", lambda g: g.tensor_copy(out=kvbf[:], in_=kvb[:]),
                   reads=["kv"], writes=["kvbf"])
            for half in range(2):
                b = PF[4 + half]
                bk = fkey[4 + half]
                sch.op("act", lambda s, c=c, h=half, b=b:
                       s.copy(out=OBF[:, c, h * 512:(h + 1) * 512], in_=b[:]),
                       reads=[bk], writes=["OBF"])
                for q in (2 * half, 2 * half + 1):
                    ln_stats(sl2, c, q, OBF[:, c, q * 256:(q + 1) * 256], "OBF")

        def tile(i):
            xsl = xs[i % 3]
            xkey = f"x{i % 3}"
            have1 = i + 1 < ntiles
            have2 = i + 2 < ntiles
            sl2 = sl2_slots.pop(i)

            if i == 0:
                ln_finish(sl2)

            def ln2_out(v, tg, m, r):
                return v.tensor_scalar(out=A8[:, tg, :], in0=OBF[:, tg, :],
                                       scalar1=m, scalar2=r, op0=ALU.subtract, op1=ALU.mult)
            for tg in range(2):
                ln_apply(sl2, tg, ln2_out, ["OBF"], "A8")
            H2 = E // 2
            for tg in range(2):
                for eng, lo in (("pool", 0), ("dve", H2)):
                    sch.op(eng, lambda x, tg=tg, lo=lo: x.tensor_tensor(
                        out=A9[:, tg, lo:lo + H2], in0=A8[:, tg, lo:lo + H2],
                        in1=gam_s[:, lo:lo + H2], op=ALU.mult),
                        reads=["A8", "gamma"], writes=["A9"])
                for eng, lo in (("pool", 0), ("dve", H2)):
                    sch.op(eng, lambda x, tg=tg, lo=lo: x.tensor_tensor(
                        out=A8[:, tg, lo:lo + H2], in0=A9[:, tg, lo:lo + H2],
                        in1=bet_s[:, lo:lo + H2], op=ALU.add),
                        reads=["A9", "beta"], writes=["A8"])
                for eng, lo in (("pool", 0), ("dve", H2)):
                    sch.op(eng, lambda x, tg=tg, lo=lo: x.tensor_tensor(
                        out=A9[:, tg, lo:lo + H2], in0=A8[:, tg, lo:lo + H2],
                        in1=A7[:, tg, lo:lo + H2], op=ALU.mult),
                        reads=["A8", "A7"], writes=["A9"])

            for tg in range(2):
                for fb in range(8):
                    sch.op("pe", lambda t, tg=tg, fb=fb:
                           t.transpose(out=PT[tg][:, fb, :],
                                       in_=A9[:, tg, fb * 128:(fb + 1) * 128],
                                       identity=ident_s[:]),
                           reads=["A9", "ident"], writes=[tkey[tg]], inc=(fb == 7))
            for tg in range(2):
                sch.op("dve", lambda v, tg=tg:
                       v.tensor_copy(out=A2[:, :, tg * 128:(tg + 1) * 128], in_=PT[tg][:]),
                       reads=[tkey[tg]], writes=["A2"])

            if have1:
                attn_front(i + 1, 0)

            sl3 = new_slot()
            for tg in range(2):
                for half in range(2):
                    bank = PF[(tg * 2 + half) % 2]
                    bkey = fkey[(tg * 2 + half) % 2]
                    for fc2 in range(8):
                        sch.op("pe", lambda t, tg=tg, h=half, fc=fc2, b=bank:
                               t.matmul(out=b[:],
                                        lhsT=A2[:, fc, tg * 128:(tg + 1) * 128],
                                        rhs=wo[:, fc, h * 512:(h + 1) * 512],
                                        start=(fc == 0), stop=(fc == 7)),
                               reads=["A2", "wo"], writes=[bkey], inc=(fc2 == 7))
                    sch.op("act", lambda s, tg=tg, h=half, b=bank:
                           s.copy(out=R2b[:, tg, h * 512:(h + 1) * 512], in_=b[:]),
                           reads=[bkey], writes=["R2b"])
                    for q in (2 * half, 2 * half + 1):
                        ln_stats(sl3, tg, q, R2b[:, tg, q * 256:(q + 1) * 256], "R2b")

            ln_finish(sl3)

            if have1:
                stage_a2(i + 1)

            sl4 = new_slot()

            def lnr1_out(v, tg, m, r):
                return v.tensor_scalar(out=A8[:, tg, :], in0=R2b[:, tg, :],
                                       scalar1=m, scalar2=r, op0=ALU.subtract, op1=ALU.mult)
            for tg in range(2):
                ln_apply(sl3, tg, lnr1_out, ["R2b"], ["A8", "r1a0"] if tg == 0 else "A8")
                for eng, lo, w in (("pool", 0, E // 4), ("dve", E // 4, 3 * E // 4)):
                    sch.op(eng, lambda x, tg=tg, lo=lo, w=w: x.tensor_tensor(
                        out=xsl[:, tg, lo:lo + w], in0=xsl[:, tg, lo:lo + w],
                        in1=A8[:, tg, lo:lo + w], op=ALU.add),
                        reads=[xkey, "A8"], writes=[xkey])
                for q in range(4):
                    ln_stats(sl4, tg, q, xsl[:, tg, q * 256:(q + 1) * 256], xkey)
            if have2:
                sch.op("syncq", lambda s, j=i + 2:
                       s.dma_start(out=xs[j % 3][:], in_=x_tiles[j]), writes=[f"x{(i + 2) % 3}"])

            ln_finish(sl4)

            def ln3_out(v, tg, m, r):
                return v.tensor_scalar(out=A1[:, tg, :], in0=xsl[:, tg, :],
                                       scalar1=m, scalar2=r, op0=ALU.subtract, op1=ALU.mult)
            for tg in range(2):
                ln_apply(sl4, tg, ln3_out, [xkey], ["A1", ("A1t", tg)])

            if have1:
                sl2_slots[i + 1] = new_slot()
                attn_back(i + 1, 0, sl2_slots[i + 1], tail=False)
            sl5 = new_slot()

            for tg in range(2):
                for fb in range(8):
                    sch.op("pe", lambda t, tg=tg, fb=fb:
                           t.transpose(out=PT[tg][:, fb, :],
                                       in_=A1[:, tg, fb * 128:(fb + 1) * 128],
                                       identity=ident_s[:]),
                           reads=[("A1t", tg), "ident"], writes=[tkey[tg]], inc=(fb == 7))
            for tg in range(2):
                sch.op("dve", lambda v, tg=tg:
                       v.tensor_copy(out=ln3fm[:, :, 1 + tg * 128:1 + (tg + 1) * 128], in_=PT[tg][:]),
                       reads=[tkey[tg]], writes=["ln3"])

            if have1:
                attn_front(i + 1, 1, defer=True)

            d8 = A2[:]
            xr = A8[:].rearrange("p g (c t) -> p (g c) t", c=4)
            xk = A9[:].rearrange("p g (c t) -> p (g c) t", c=4)
            cur = ln3fm[:, :, 1:T + 1]
            prev = ln3fm[:, :, 0:T]
            murb = mur_s[:].unsqueeze(2).broadcast_to((128, 8, T))
            k_r = A1[:].rearrange("p g (c t) -> p (g c) t", c=4)

            def k2_mm(dp, dc, fcs, inc_last=False):
                bank = PF[dp % 2]
                bkey = fkey[dp % 2]
                last = fcs[-1]
                for fc2 in fcs:
                    sch.op("pe", lambda t, dc=dc, fc=fc2, b=bank:
                           t.matmul(out=b[:, (dc % 2) * 256:(dc % 2) * 256 + 256],
                                    lhsT=wkT[:, fc, dc * 128:(dc + 1) * 128],
                                    rhs=xk[:, fc, :],
                                    start=(fc == 0), stop=(fc == 7)),
                           reads=["wkT", "A9"], writes=[bkey],
                           inc=(fc2 == 7 and dc % 2 == 1) or (inc_last and fc2 == last))

            def k2_relu(dp):
                bank = PF[dp % 2]
                bkey = fkey[dp % 2]
                sch.op("act", lambda s, dp=dp, b=bank:
                       s.activation(out=k_r[:, 2 * dp:2 * dp + 2, :],
                                    in_=b[:].rearrange("p (a b) -> p a b", a=2), func=AF.Relu),
                       reads=[bkey], writes=["A1", ("A1t", 0), ("A1t", 1)])

            for hlf in range(2):
                fsl = slice(4 * hlf, 4 * hlf + 4)
                mukb = muk_s[:, fsl].unsqueeze(2).broadcast_to((128, 4, T))
                sch.op("dve", lambda v, fsl=fsl: v.tensor_tensor(
                    out=d8[:, fsl], in0=cur[:, fsl], in1=prev[:, fsl], op=ALU.subtract),
                    reads=["ln3"], writes=["A2"])
                sch.op("dve", lambda v, fsl=fsl, mukb=mukb: v.tensor_tensor(
                    out=xk[:, fsl], in0=d8[:, fsl], in1=mukb, op=ALU.mult),
                    reads=["A2", "muk"], writes=["A9"])
                sch.op("dve", lambda v, fsl=fsl: v.tensor_tensor(
                    out=xk[:, fsl], in0=xk[:, fsl], in1=prev[:, fsl], op=ALU.add),
                    reads=["A9", "ln3"], writes=["A9"])
                if have1:
                    front_mask(2 + hlf)
                if hlf == 0:
                    k2_mm(0, 0, list(range(4)), inc_last=True)
            k2_mm(0, 0, range(4, 8))
            k2_mm(0, 1, range(8))
            k2_relu(0)

            sch.op("dve", lambda v: v.tensor_tensor(out=xr, in0=d8, in1=murb, op=ALU.mult),
                   reads=["A2", "mur"], writes=["A8"])
            sch.op("dve", lambda v: v.tensor_tensor(out=xr, in0=xr, in1=prev, op=ALU.add),
                   reads=["A8", "ln3"], writes=["A8"])
            if have1:
                front_mask(0)
                front_mask(1)
                front_qk_dve(1)
            sch.op("pool", lambda g: g.tensor_copy(out=ln3fm[:, :, 0:1], in_=ln3fm[:, :, T:T + 1]),
                   reads=["ln3"], writes=["ln3"], small=True)

            for dp in (1, 2, 3):
                for dc in (2 * dp, 2 * dp + 1):
                    k2_mm(dp, dc, range(8))
                k2_relu(dp)
            sch.op("pool", lambda g: g.tensor_tensor(
                out=A2[:].rearrange("p a b -> p (a b)"),
                in0=k_r.rearrange("p a b -> p (a b)"),
                in1=k_r.rearrange("p a b -> p (a b)"), op=ALU.mult),
                reads=["A1"], writes=["A2"])

            for tg in range(2):
                for half in range(2):
                    bank = PF[2 + (tg * 2 + half) % 2]
                    bkey = fkey[2 + (tg * 2 + half) % 2]
                    for fc2 in range(8):
                        sch.op("pe", lambda t, tg=tg, h=half, fc=fc2, b=bank:
                               t.matmul(out=b[:],
                                        lhsT=xr[:, fc, tg * 128:(tg + 1) * 128],
                                        rhs=wrT[:, fc, h * 512:(h + 1) * 512],
                                        start=(fc == 0), stop=(fc == 7)),
                               reads=["A8", "wrT"], writes=[bkey], inc=(fc2 == 7))
                    sch.op("act", lambda s, tg=tg, h=half, b=bank:
                           s.activation(out=A9[:, tg, h * 512:(h + 1) * 512], in_=b[:],
                                        func=AF.Tanh, scale=0.5),
                           reads=[bkey], writes=["A9"])
                    sch.op("act", lambda s, tg=tg, h=half:
                           s.activation(out=A9[:, tg, h * 512:(h + 1) * 512],
                                        in_=A9[:, tg, h * 512:(h + 1) * 512],
                                        func=AF.Identity, scale=0.5, bias=halfb[:]),
                           reads=["A9", "half"], writes=["A9"])

            if have1:
                attn_back_tail(0, sl2_slots[i + 1])
            if have2:
                ln1_prestats(i + 2)
                ln1_apply(i + 2)
            if have1:
                attn_back(i + 1, 1, sl2_slots[i + 1])

            y2banks = (4, 5, 2, 3)
            for tg in range(2):
                for half in range(2):
                    bank = PF[y2banks[tg * 2 + half]]
                    bkey = fkey[y2banks[tg * 2 + half]]
                    for fc2 in range(8):
                        sch.op("pe", lambda t, tg=tg, h=half, fc=fc2, b=bank:
                               t.matmul(out=b[:],
                                        lhsT=A2[:, fc, tg * 128:(tg + 1) * 128],
                                        rhs=wvT[:, fc, h * 512:(h + 1) * 512],
                                        start=(fc == 0), stop=(fc == 7)),
                               reads=["A2", "wvT"], writes=[bkey], inc=(fc2 == 7))
                    sch.op("dve", lambda v, tg=tg, h=half, b=bank:
                           v.tensor_tensor(out=R2b[:, tg, h * 512:(h + 1) * 512], in0=b[:],
                                           in1=A9[:, tg, h * 512:(h + 1) * 512], op=ALU.mult),
                           reads=[bkey, "A9"], writes=["R2b"])
                    for q in (2 * half, 2 * half + 1):
                        ln_stats(sl5, tg, q, R2b[:, tg, q * 256:(q + 1) * 256], "R2b")

            if have2:
                stage_a_ln(i + 2, do_apply=False)
                stage_a(i + 2)

            if have1 and sl2_slots[i + 1] + 1 == sl5:
                ln_finish2(sl2_slots[i + 1], sl5)
            else:
                if have1:
                    ln_finish(sl2_slots[i + 1])
                ln_finish(sl5)

            def lnr2_out(v, tg, m, r):
                return v.tensor_scalar(out=A8[:, tg, :], in0=R2b[:, tg, :],
                                       scalar1=m, scalar2=r, op0=ALU.subtract, op1=ALU.mult)
            for tg in range(2):
                ln_apply(sl5, tg, lnr2_out, ["R2b"], "A8")
                for eng, lo, w in (("pool", 0, E // 4), ("dve", E // 4, 3 * E // 4)):
                    sch.op(eng, lambda x, tg=tg, lo=lo, w=w: x.tensor_tensor(
                        out=xsl[:, tg, lo:lo + w], in0=xsl[:, tg, lo:lo + w],
                        in1=A8[:, tg, lo:lo + w], op=ALU.add),
                        reads=[xkey, "A8"], writes=[xkey])
            sch.op("syncq", lambda s, i=i: s.dma_start(out=out_tiles[i], in_=xsl[:]),
                   reads=[xkey], writes=[])

        ln1_prestats(0)
        stage_a_ln(0)
        stage_a(0)
        stage_a2(0)
        attn_front(0, 0)
        sl2_slots[0] = new_slot()
        attn_back(0, 0, sl2_slots[0])
        attn_front(0, 1)
        attn_back(0, 1, sl2_slots[0])
        if ntiles > 1:
            sch.op("syncq", lambda s: s.dma_start(out=xs[1][:], in_=x_tiles[1]),
                   writes=["x1"])
            ln1_prestats(1)
            stage_a_ln(1)
            stage_a(1)
        for i in range(ntiles):
            tile(i)

        final = sch.cnt["syncq"]
        sch.raw("sync", lambda s: s.wait_ge(s_syncq, final))

        @block.tensor
        def _(t):
            for fn in sch.progs["pe"]:
                fn(t)

        @block.vector
        def _(v):
            for fn in sch.progs["dve"]:
                fn(v)

        @block.scalar
        def _(s):
            for fn in sch.progs["act"]:
                fn(s)

        @block.gpsimd
        def _(g):
            for fn in sch.progs["pool"]:
                fn(g)

        @block.sync
        def _(s):
            for fn in sch.progs["sync"]:
                fn(s)

    return nc



def kernel(L0_attn_x, L0_attn_Wq, L0_attn_Wk, L0_attn_Wv, L0_attn_Wo,
           L0_attn_gamma, L0_attn_beta, L0_attn_Wg1, L0_attn_Wg2,
           L0_ffn_W_r, L0_ffn_W_k, L0_ffn_W_v, L0_ffn_mu_r, L0_ffn_mu_k):
    x = np.ascontiguousarray(np.asarray(L0_attn_x, dtype=np.float32))
    B = x.shape[0]
    tabs = host_tables()
    gamma_b = np.ascontiguousarray(np.broadcast_to(
        np.asarray(L0_attn_gamma, np.float32)[None, :], (128, E)))
    beta_b = np.ascontiguousarray(np.broadcast_to(
        np.asarray(L0_attn_beta, np.float32)[None, :], (128, E)))
    mur = np.ascontiguousarray(np.asarray(L0_ffn_mu_r, np.float32).reshape(8, 128).T)
    muk = np.ascontiguousarray(np.asarray(L0_ffn_mu_k, np.float32).reshape(8, 128).T)

    shared = {
        "Wq": np.asarray(L0_attn_Wq, np.float32), "Wk": np.asarray(L0_attn_Wk, np.float32),
        "Wv": np.asarray(L0_attn_Wv, np.float32), "Wo": np.asarray(L0_attn_Wo, np.float32),
        "Wg1": np.asarray(L0_attn_Wg1, np.float32),
        "Wg2": np.concatenate([np.asarray(L0_attn_Wg2, np.float32),
                               np.zeros((64, E), np.float32)], axis=0),
        "W_r": np.asarray(L0_ffn_W_r, np.float32), "W_k2": np.asarray(L0_ffn_W_k, np.float32),
        "W_v2": np.asarray(L0_ffn_W_v, np.float32),
        "gammab": gamma_b, "betab": beta_b, "mur": mur, "muk": muk,
        "maskT": tabs["maskT"], "lamq": tabs["lamq"], "lamk": tabs["lamk"],
        "lamc_bc": tabs["lamc_bc"], "ident": tabs["ident"],
    }
    nc = build_program()
    in_maps = [dict(shared, x=np.ascontiguousarray(x[b])) for b in range(B)]
    res = run_bass_kernel_spmd(nc, in_maps, core_ids=list(range(B)))
    out = np.stack([res.results[b]["out"] for b in range(B)], axis=0)
    return out
